# version 10
# baseline (speedup 1.0000x reference)
from contextlib import ExitStack

import numpy as np
import ml_dtypes

import concourse.bass as bass
import concourse.mybir as mybir
from concourse.bass_utils import run_bass_kernel_spmd

F32 = mybir.dt.float32
BF16 = mybir.dt.bfloat16
U32 = mybir.dt.uint32
AF = mybir.ActivationFunctionType
ALU = mybir.AluOpType
AX = mybir.AxisListType

T = 4096
D = 1024
NT = 32
H = 16
HD = 64
DFF = 4096
NE = 8
CAP = 512
EPS = 1e-5
NEG = -1e9
OOBTOK = 1000000

Q_LO = [0, 1, 2, 3, 8, 9, 10, 11]
Q_HI = [4, 5, 6, 7, 12, 13, 14, 15]
Q_REG = {h: r for r, h in enumerate(Q_LO)}
Q_REG.update({h: r for r, h in enumerate(Q_HI)})

ENGCH = {"pe": "pe", "act": "act", "dve": "dve", "pool": "pool", "sp": "sp"}


def issuer_of(ch):
    if ch in ENGCH:
        return ch
    if ch.startswith("qs_"):
        return "sp"
    if ch.startswith("qa_"):
        return "act"
    return "pool"


FINAL_WAIT = {"qp_init"}


class Op:
    __slots__ = ("ch", "emit", "deps", "needs_inc", "val", "tag")

    def __init__(self, ch, emit, tag):
        self.ch = ch
        self.emit = emit
        self.deps = []
        self.needs_inc = False
        self.val = 0
        self.tag = tag


class Prog:
    def __init__(self):
        self.stream = {e: [] for e in ("pe", "act", "dve", "pool", "sp")}
        self.res = {}

    def op(self, ch, emit, reads=(), writes=(), tag=""):
        o = Op(ch, emit, tag)
        dep = {}
        for k in reads:
            e = self.res.get(k)
            if e is not None and e[0] is not None:
                dep[id(e[0])] = e[0]
        for k in writes:
            e = self.res.get(k)
            if e is not None:
                if e[0] is not None:
                    dep[id(e[0])] = e[0]
                for r in e[1]:
                    dep[id(r)] = r
        for d in dep.values():
            if d is o:
                continue
            if d.ch == "pe" and ch == "pe":
                continue
            o.deps.append(d)
        for k in reads:
            self.res.setdefault(k, [None, []])[1].append(o)
        for k in writes:
            self.res[k] = [o, []]
        self.stream[issuer_of(ch)].append(o)
        return o

    def alias(self, dst, *srcs):
        ent = self.res.setdefault(dst, [None, []])
        for s in srcs:
            e = self.res.get(s)
            if e is not None:
                if e[0] is not None:
                    ent[1].append(e[0])
                ent[1].extend(e[1])

    def finalize(self):
        for eng in self.stream:
            for o in self.stream[eng]:
                for d in o.deps:
                    d.needs_inc = True
        counts = {}
        for eng in self.stream:
            for o in self.stream[eng]:
                if o.ch.startswith("q"):
                    o.needs_inc = True
                if o.needs_inc:
                    counts[o.ch] = counts.get(o.ch, 0) + (16 if o.ch.startswith("q") else 1)
                    o.val = counts[o.ch]
        self.final_counts = counts

    def emit_engine(self, eng, engobj, sems):
        clock = {}
        for o in self.stream[eng]:
            for d in o.deps:
                v = self.final_counts[d.ch] if d.ch in FINAL_WAIT else d.val
                if clock.get(d.ch, 0) < v:
                    engobj.wait_ge(sems[d.ch], v)
                    clock[d.ch] = v
            try:
                inst = o.emit(engobj)
            except Exception:
                print("EMIT FAILED at op:", o.tag, o.ch)
                raise
            if o.needs_inc:
                inst.then_inc(sems[o.ch], 16 if o.ch.startswith("q") else 1)


CONST_SPECS = {
    "ident": ([128, 128], BF16), "u128": ([128, 128], BF16),
    "onesrow": ([1, 128], BF16), "onesrowf": ([1, 128], F32),
    "onescol": ([128, 1], BF16),
    "m0": ([128, 128], BF16), "m2": ([128, 128], BF16),
    "cost": ([128, NT * 32], BF16), "sint": ([128, NT * 32], BF16),
    "iota8": ([128, 8], F32), "tokids": ([128, NT], U32),
    "epsb": ([128, 1], F32),
}


def host_consts():
    bf = ml_dtypes.bfloat16
    c = {}
    c["ident"] = np.eye(128, dtype=bf)
    c["u128"] = np.triu(np.ones((128, 128), np.float32)).astype(bf)
    c["onesrow"] = np.ones((1, 128), bf)
    c["onesrowf"] = np.ones((1, 128), np.float32)
    c["onescol"] = np.ones((128, 1), bf)
    kk = np.arange(128)[:, None]
    qq = np.arange(128)[None, :]
    c["m0"] = np.where(kk <= qq, 0.0, NEG).astype(np.float32).astype(bf)
    c["m2"] = np.where(qq < kk, 0.0, NEG).astype(np.float32).astype(bf)
    half = HD // 2
    theta = 1.0 / 10000.0 ** (np.arange(half, dtype=np.float32) / half)
    fr = np.outer(np.arange(T, dtype=np.float32), theta).astype(np.float32)
    cosv = np.cos(fr).astype(np.float32).reshape(NT, 128, half).transpose(1, 0, 2)
    sinv = np.sin(fr).astype(np.float32).reshape(NT, 128, half).transpose(1, 0, 2)
    c["cost"] = np.ascontiguousarray(cosv.reshape(128, NT * half)).astype(bf)
    c["sint"] = np.ascontiguousarray(sinv.reshape(128, NT * half)).astype(bf)
    c["iota8"] = np.tile(np.arange(8, dtype=np.float32), (128, 1))
    c["epsb"] = np.full((128, 1), EPS, np.float32)
    c["tokids"] = np.ascontiguousarray(
        (np.arange(NT, dtype=np.uint32)[None, :] * 128
         + np.arange(128, dtype=np.uint32)[:, None]))
    return c


def build(debug=False):
    nc = bass.Bass("TRN2", target_bir_lowering=False)
    P = Prog()

    dx = nc.dram_tensor("x", [T, D], F32, kind="ExternalInput").ap()
    dW = {}
    for nm, shp in [("Wq", [D, D]), ("bq", [1, D]), ("Wk", [D, 256]), ("bk", [1, 256]),
                    ("Wv", [D, 256]), ("bv", [1, 256]), ("Wo", [D, D]), ("bo", [1, D]),
                    ("g1", [128, 8]), ("b1", [128, 8]), ("g2", [128, 8]), ("b2", [128, 8]),
                    ("Wf1", [D, DFF]), ("bf1c", [128, 32]), ("Wf2", [DFF, D]), ("bf2", [1, D]),
                    ("Wr", [D, NE]), ("wi2", [NE * D, DFF]), ("wo2", [NE * DFF, D])]:
        dW[nm] = nc.dram_tensor(nm, shp, F32, kind="ExternalInput").ap()
    dC = {nm: nc.dram_tensor("dc_" + nm, shp, dt, kind="ExternalInput").ap()
          for nm, (shp, dt) in CONST_SPECS.items()}
    dout = nc.dram_tensor("out", [T, D], F32, kind="ExternalOutput").ap()
    d_xn = nc.dram_tensor("scr_xn", [T, D], BF16, kind="Internal").ap()
    d_x2 = nc.dram_tensor("scr_x2", [T, D], F32, kind="Internal").ap()
    d_b2s = nc.dram_tensor("scr_base2", [T, D], BF16, kind="Internal").ap()
    d_s = nc.dram_tensor("scr_s", [T, D], F32, kind="Internal").ap()
    d_wf1p = nc.dram_tensor("scr_wf1p", [128, 8 * DFF], BF16, kind="Internal").ap()
    d_tok = nc.dram_tensor("scr_tok", [NE * CAP, 1], U32, kind="Internal").ap()
    d_eo = nc.dram_tensor("scr_eo", [NE * CAP, D], BF16, kind="Internal").ap()
    dbg = {}
    if debug:
        for nm, shp, dt in [("Wq", [128, 8 * 1024], BF16), ("Wo", [128, 8 * 1024], BF16),
                            ("Wkv", [128, 8 * 512], BF16), ("brow", [1, 1536], BF16),
                            ("qtmp", [128, D + 256], BF16), ("ropeq", [128, D], BF16),
                            ("ropek", [128, 256], BF16), ("qT", [128, 8 * 128], BF16),
                            ("kT", [128, 256], BF16), ("vsb", [128, 260], BF16),
                            ("attn_out", [128, D], BF16), ("attnT", [128, 1536], BF16),
                            ("xn", [T, D], BF16), ("logits", [128, NT * 8], F32),
                            ("x2", [T, D], F32), ("s", [T, D], F32),
                            ("idx1", [128, NT], U32), ("idx2", [128, NT], U32),
                            ("p1", [128, NT], F32), ("p2", [128, NT], F32),
                            ("tok", [NE * CAP, 1], U32), ("eo", [NE * CAP, D], BF16)]:
            dbg[nm] = nc.dram_tensor("dbg_" + nm, shp, dt, kind="ExternalOutput").ap()

    ctx = ExitStack()
    sbt = lambda name, shape, dt: ctx.enter_context(nc.sbuf_tensor(name, shape, dt))
    pst = lambda name, shape, dt: ctx.enter_context(nc.psum_tensor(name, shape, dt))

    C = {nm: sbt("c_" + nm, *CONST_SPECS[nm]) for nm in CONST_SPECS}
    gb = {nm: sbt("v_" + nm, [128, 8], F32) for nm in ("g1", "b1", "g2", "b2")}

    gT = sbt("gT", [128, 32 * 512], BF16)
    stage = gT[:, :].bitcast(F32)
    Wq_s = sbt("Wq_s", [128, 8 * 1024], BF16)
    Wo_s = sbt("Wo_s", [128, 8 * 1024], BF16)
    Wkv_s = sbt("Wkv_s", [128, 8 * 512], BF16)
    b2T = sbt("b2T", [128, 8 * 512], BF16)
    Wr_s = sbt("Wr_s", [128, 8 * 8], BF16)
    wf1buf = [sbt(f"wf1b{i}", [128, 8 * 512], BF16) for i in range(2)]
    ssb = [sbt(f"ssb{i}", [128, D], F32) for i in range(4)]

    x_sb = [sbt(f"x_sb{i}", [128, D], F32) for i in range(2)]
    x2sb = [sbt(f"x2sb{i}", [128, D], F32) for i in range(2)]
    xn_sb = [sbt(f"xn_{i}", [128, D], BF16) for i in range(4)]
    stat = [sbt(f"stat{i}", [128, 24], F32) for i in range(4)]
    musd = [sbt(f"musd{i}", [128, 8], F32) for i in range(4)]
    base1 = [sbt(f"base1_{i}", [128, D], BF16) for i in range(2)]
    baseT = [sbt(f"baseT{i}", [128, D], BF16) for i in range(2)]
    qtmp = [sbt(f"qtmp{i}", [128, D + 256], BF16) for i in range(2)]
    ropeq = [sbt(f"ropeq{i}", [128, D], BF16) for i in range(2)]
    ropek = [sbt(f"ropek{i}", [128, 256], BF16) for i in range(2)]
    tmpA = [sbt(f"tmpA{i}", [128, 512], BF16) for i in range(2)]
    tmpB = [sbt(f"tmpB{i}", [128, 512], BF16) for i in range(2)]
    krA = [sbt(f"krA{i}", [128, 128], BF16) for i in range(2)]
    krB = [sbt(f"krB{i}", [128, 128], BF16) for i in range(2)]
    qT = [sbt(f"qT{i}", [128, 8 * 128], BF16) for i in range(2)]
    kT = [sbt(f"kT{i}", [128, 256], BF16) for i in range(4)]
    vsb = [sbt(f"v{i}", [128, 4 * 65], BF16) for i in range(4)]
    attnT = [sbt(f"attnT{i}", [128, 3 * 512], BF16) for i in range(2)]
    rrec = [sbt(f"rrec{i}", [128, 4], F32) for i in range(2)]
    attn_out = [sbt(f"attn_out{i}", [128, D], BF16) for i in range(2)]
    aoT = [sbt(f"aoT{i}", [128, D], BF16) for i in range(2)]
    base2 = [sbt(f"base2_{i}", [128, D], BF16) for i in range(2)]
    logits_all = sbt("logits_all", [128, NT * 8], F32)
    muS = [sbt(f"muS{i}", [128, 8], F32) for i in range(4)]

    brow = sbt("brow", [1, 1536], BF16)
    borow = sbt("borow", [1, D], BF16)
    bf1col = sbt("bf1col", [128, 32], F32)
    bf2row = sbt("bf2row", [1, D], BF16)
    Sbc = sbt("Sbc", [128, 8], F32)
    srow = ssb[0][0:1, 0:512]
    srow2 = ssb[0][0:1, 512:520]
    bf1tmp = sbt("bf1tmp", [128, 32], F32)
    tokfill = sbt("tokfill", [32, 128], U32)
    tokcol = [sbt(f"tokcol{i}", [128, 1], U32) for i in range(2)]

    m1_all = sbt("m1_all", [128, NT * 8], BF16)
    m2_all = sbt("m2_all", [128, NT * 8], BF16)
    rt_f = sbt("rt_f", [128, NT * 8], F32)
    rt_g = sbt("rt_g", [128, NT * 8], F32)
    rt_h = sbt("rt_h", [128, NT * 8], F32)
    max1 = sbt("max1", [128, NT], F32)
    max2 = sbt("max2", [128, NT], F32)
    zrec = sbt("zrec", [128, NT], F32)
    ecol = sbt("ecol", [128, NT], F32)
    poscol = sbt("poscol", [128, NT], F32)
    bumpc = sbt("bumpc", [128, NT], F32)
    idxf = sbt("idxf", [128, NT], F32)
    p1sb = sbt("p1sb", [128, NT], F32)
    p2sb = sbt("p2sb", [128, NT], F32)
    idxu = [sbt(f"idxu{i}", [128, NT], U32) for i in range(2)]

    stg_o = b2T
    wf2buf = [Wo_s[:, 0:2048], Wo_s[:, 2048:4096]]
    b2blk = [base1[0], base1[1], baseT[0], baseT[1]]
    b2Tb = [b2T, Wkv_s]
    x2blk = [x2sb[0], x2sb[1], x_sb[0], x_sb[1]]
    eib = [Wkv_s[:, 0:1024], Wkv_s[:, 1024:2048]]
    eiTb = [Wq_s[:, 0:4096], Wo_s[:, 0:4096]]
    wib = [b2T, Wq_s[:, 4096:8192]]
    eob4 = [base1[0], base1[1], baseT[0], baseT[1]]
    wob = [Wo_s[:, 4096:6144], Wo_s[:, 6144:8192]]
    gset = [[Wo_s[:, (4 * h + k) * 1024:(4 * h + k + 1) * 1024]
             for k in range(4)] for h in range(2)]
    moes = [ssb[0], ssb[1]]
    gsil = [ssb[2], ssb[3]]
    sfin = x2sb

    pT = pst("pT", [128, 1024], BF16)
    B = [pst(f"B{i}", [128, 512], F32) for i in range(7)]

    def dma(q, out, in_, reads, writes, tag=""):
        return P.op(q, lambda e, o=out, i=in_: e.dma_start(out=o, in_=i), reads, writes, tag)

    def mm(out, lhsT, rhs, reads, writes, start, stop, tag=""):
        P.op("pe", lambda e, o=out, l=lhsT, r=rhs, s1=start, s2=stop:
             e.matmul(o, l, r, start=s1, stop=s2, skip_group_check=True),
             reads, writes, tag)

    def transp(out, in_, reads, writes, tp=None, tag=""):
        P.op("pe", lambda e, o=out, i=in_, t=tp:
             e.transpose(o, i, C["ident"][:, :], tile_position=t),
             tuple(reads) + ("c_ident",), writes, tag)

    def act(out, in_, func, reads, writes, bias=0.0, scale=1.0, accum=None, tag=""):
        P.op("act", lambda e, o=out, i=in_, f=func, b=bias, s=scale, a=accum:
             e.activation(o, i, f, bias=b, scale=s, accum_out=a), reads, writes, tag)

    def tt(out, in0, in1, op, reads, writes, tag="", ch="dve"):
        P.op(ch, lambda e, o=out, a=in0, b=in1, p=op: e.tensor_tensor(o, a, b, p),
             reads, writes, tag)

    def ts(out, in0, s1, op0, reads, writes, s2=None, op1=None, tag="", ch="dve"):
        if op1 is None:
            P.op(ch, lambda e, o=out, i=in0, a=s1, p=op0:
                 e.tensor_scalar(o, i, a, None, p), reads, writes, tag)
        else:
            P.op(ch, lambda e, o=out, i=in0, a=s1, b=s2, p=op0, p2=op1:
                 e.tensor_scalar(o, i, a, b, p, p2), reads, writes, tag)

    def stt(out, in0, scal, in1, op0, op1, reads, writes, tag="", ch="dve"):
        P.op(ch, lambda e, o=out, i=in0, s=scal, j=in1, p=op0, p2=op1:
             e.scalar_tensor_tensor(o, i, s, j, p, p2), reads, writes, tag)

    def red(out, in_, op, reads, writes, tag="", ch="dve"):
        P.op(ch, lambda e, o=out, i=in_, p=op: e.tensor_reduce(o, i, AX.X, p),
             reads, writes, tag)

    def cp(out, in_, reads, writes, tag="", ch="dve"):
        P.op(ch, lambda e, o=out, i=in_: e.tensor_copy(o, i), reads, writes, tag)

    def evac(out, in_, reads, writes, c, tag=""):
        if c % 2 == 0:
            act(out, in_, AF.Copy, reads, writes, tag=tag)
        else:
            cp(out, in_, reads, writes, tag=tag, ch="dve")

    def memset(buf_ap, val, reads, writes, tag=""):
        P.op("pool", lambda e, o=buf_ap, v=val: e.memset(o, v), reads, writes, tag)

    def bn(out, in_, reads, writes, tag=""):
        P.op("dve", lambda e, o=out, i=in_: e.bn_stats(o, i), reads, writes, tag)

    def bnag(out, in_, reads, writes, tag=""):
        P.op("dve", lambda e, o=out, i=in_: e.bn_aggr(o, i), reads, writes, tag)

    breg = nc.gpsimd.alloc_register("bnd")
    P.op("pool", lambda e: e.reg_mov(breg, NE * CAP - 1), (), ("breg",), "bndmov")
    for nm in CONST_SPECS:
        dma("qsp", C[nm][:, :], dC[nm], (), ("c_" + nm,), f"ld_{nm}")
    for nm in gb:
        dma("qsp", gb[nm][:, :], dW[nm], (), ("v_" + nm,))
    dma("qp_init", Wo_s[:, :].rearrange("p (c n) -> p c n", c=8),
        dW["Wo"].rearrange("(c p) n -> p c n", p=128), (), ("Wo_s",))
    dma("qp_init", Wr_s[:, :].rearrange("p (c n) -> p c n", c=8),
        dW["Wr"].rearrange("(c p) n -> p c n", p=128), (), ("Wr_s",))
    dma("qp_init", borow[:, :], dW["bo"], (), ("borow",))
    dma("qp_init", bf2row[:, :], dW["bf2"], (), ("bf2row",))
    for i in range(4):
        memset(bass.AP(vsb[i], 64, [vsb[i][:, :].ap[0], [65, 4], [1, 1]]), 1.0,
               (), (f"v{i}",), f"vones{i}")
    memset(tokfill[:, :], OOBTOK, (), ("tokfill",), "tokfill")
    dma("qp_init", d_tok.rearrange("(a p) o -> a (p o)", p=128), tokfill[:, :],
        ("tokfill",), ("d_tok_fill",), "tokinit")

    def prep_weight(wname, bname, ncols, out_sb, col_of, brow_off, scale=None):
        dma("qsp", stage[:, 0:8 * ncols].rearrange("p (c n) -> p c n", c=8),
            dW[wname].rearrange("(c p) n -> p c n", p=128),
            ("stage",), ("stage",), f"stg_{wname}")
        for c in range(8):
            ts(out_sb[:, col_of(c):col_of(c) + ncols],
               stage[:, c * ncols:(c + 1) * ncols], gb["g1"][:, c:c + 1], ALU.mult,
               ("stage", "v_g1"), (f"W_{wname}",), s2=scale,
               op1=(None if scale is None else ALU.mult), tag=f"scl_{wname}{c}")
        nch = (ncols + 511) // 512
        for jc in range(nch):
            w = min(512, ncols - jc * 512)
            for c in range(8):
                mm(B[0][0:1, 0:w], gb["b1"][:, c:c + 1],
                   stage[:, c * ncols + jc * 512: c * ncols + jc * 512 + w],
                   ("stage", "v_b1"), ("B0",), start=(c == 0), stop=(c == 7),
                   tag=f"bmm_{wname}{jc}{c}")
            dma("qsp", srow[0:1, 0:w], dW[bname][0:1, jc * 512: jc * 512 + w],
                ("srow",), ("srow",), f"ldb_{bname}{jc}")
            tt(srow[0:1, 0:w], B[0][0:1, 0:w], srow[0:1, 0:w], ALU.add,
               ("B0", "srow"), ("srow",), tag=f"badd_{wname}{jc}")
            ts(brow[0:1, brow_off + jc * 512: brow_off + jc * 512 + w], srow[0:1, 0:w],
               1.0 if scale is None else scale, ALU.mult, ("srow",),
               (f"brow_{wname}",), tag=f"bscl_{wname}{jc}")

    prep_weight("Wq", "bq", 1024, Wq_s, lambda c: c * 1024, 0, scale=0.125)
    prep_weight("Wk", "bk", 256, Wkv_s, lambda c: c * 512, 1024)
    prep_weight("Wv", "bv", 256, Wkv_s, lambda c: c * 512 + 256, 1280)

    for c in range(8):
        mm(B[1][0:1, 0:8], C["onescol"][:, :], Wr_s[:, c * 8:(c + 1) * 8],
           ("Wr_s", "c_onescol"), ("B1",), start=(c == 0), stop=(c == 7), tag=f"S{c}")
    cp(srow2[0:1, 0:8], B[1][0:1, 0:8], ("B1",), ("srow2",))
    mm(B[1][:, 0:8], C["onesrowf"][:, :], srow2[0:1, 0:8], ("srow2", "c_onesrowf"),
       ("B1",), start=True, stop=True, tag="Sbc_mm")
    cp(Sbc[:, :], B[1][:, 0:8], ("B1",), ("Sbc",))

    dma("qsp", bf1tmp[:, :], dW["bf1c"], (), ("bf1tmp",), "ldbf1c")
    for g in range(8):
        dma("qsp", stage[:, 0:8 * 512].rearrange("p (c n) -> p c n", c=8),
            dW["Wf1"][:, g * 512:(g + 1) * 512].rearrange("(c p) n -> p c n", p=128),
            ("stage",), ("stage",), f"stgWf1_{g}")
        for c in range(8):
            ts(stg_o[:, c * 512:(c + 1) * 512], stage[:, c * 512:(c + 1) * 512],
               gb["g2"][:, c:c + 1], ALU.mult, ("stage", "v_g2"), ("stg_o",),
               tag=f"sclWf1_{g}{c}")
        for jq in range(4):
            jf = g * 4 + jq
            for c in range(8):
                mm(B[2][:, jf:jf + 1], stage[:, c * 512 + jq * 128: c * 512 + (jq + 1) * 128],
                   gb["b2"][:, c:c + 1], ("stage", "v_b2"), ("B2",),
                   start=(c == 0), stop=(c == 7), tag=f"bWf1_{jf}{c}")
        dma("qact", d_wf1p.rearrange("p (c f) -> p c f", c=8)[:, :, g * 512:(g + 1) * 512],
            stg_o[:, :].rearrange("p (c f) -> p c f", c=8), ("stg_o",),
            (f"wf1p_{g}",), f"stWf1p_{g}")
    tt(bf1col[:, :], B[2][:, 0:32], bf1tmp[:, :], ALU.add, ("B2", "bf1tmp"),
       ("bf1col",), tag="bf1col")

    if debug:
        dma("qsp", dbg["Wkv"], Wkv_s[:, :], ("W_Wk", "W_Wv"), (), "dbgWkv")
        dma("qsp", dbg["brow"], brow[0:1, :], ("brow_Wq", "brow_Wk", "brow_Wv"), (), "dbgbrow")

    def ln_stats(src_keys, src_ap, s4, o_negmu, o_rstd, o_std, tag):
        st = stat[s4]
        bk = f"bn{s4}"
        bn(st[:, 0:6], src_ap[:, 0:512], src_keys + (bk,), (bk,), tag=f"{tag}_bn0")
        bn(st[:, 6:12], src_ap[:, 512:1024], src_keys + (bk,), (bk,), tag=f"{tag}_bn1")
        bnag(st[:, 12:14], st[:, 0:6], (bk,), (bk,), tag=f"{tag}_ag0")
        bnag(st[:, 14:16], st[:, 6:12], (bk,), (bk,), tag=f"{tag}_ag1")
        tt(st[:, 16:17], st[:, 12:13], st[:, 14:15], ALU.add, (bk,), (bk,),
           tag=f"{tag}_ms")
        ts(musd[s4][:, o_negmu:o_negmu + 1], st[:, 16:17], -0.5, ALU.mult,
           (bk,), (f"musd{s4}_{o_negmu}",), tag=f"{tag}_mu")
        tt(st[:, 17:18], st[:, 12:13], st[:, 14:15], ALU.subtract, (bk,), (bk,),
           tag=f"{tag}_md")
        tt(st[:, 18:19], st[:, 17:18], st[:, 17:18], ALU.mult, (bk,), (bk,),
           tag=f"{tag}_d2")
        tt(st[:, 19:20], st[:, 13:14], st[:, 15:16], ALU.add, (bk,), (bk,),
           tag=f"{tag}_vs")
        stt(st[:, 20:21], st[:, 18:19], 0.5, st[:, 19:20], ALU.mult, ALU.add,
            (bk,), (bk,), tag=f"{tag}_2v")
        act(st[:, 21:22], st[:, 20:21], AF.Ln, (bk, "c_epsb"), (bk,),
            bias=C["epsb"][:, 0:1], scale=0.5, tag=f"{tag}_ln")
        act(musd[s4][:, o_rstd:o_rstd + 1], st[:, 21:22], AF.Exp, (bk,),
            (f"musd{s4}_{o_rstd}",), scale=-0.5, tag=f"{tag}_rstd")
        if o_std is not None:
            act(musd[s4][:, o_std:o_std + 1], st[:, 21:22], AF.Exp, (bk,),
                (f"musd{s4}_{o_std}",), scale=0.5, tag=f"{tag}_std")

    def a1_ldx(j):
        dma("qsp", x_sb[j % 2][:, :], dx[j * 128:(j + 1) * 128, :],
            (), (f"x_sb{j % 2}",), f"ldx{j}")

    def a1_pre(j):
        s2, s4 = j % 2, j % 4
        xk = f"x_sb{s2}"
        ln_stats((xk,), x_sb[s2][:, :], s4, 0, 1, None, f"ln0_{j}")
        ts(xn_sb[s4][:, :], x_sb[s2][:, :], musd[s4][:, 0:1], ALU.add,
           (xk, f"musd{s4}_0", f"musd{s4}_1"), (f"xnsb{s4}",),
           s2=musd[s4][:, 1:2], op1=ALU.mult, tag=f"xn{j}")
        dma("qact", d_xn[j * 128:(j + 1) * 128, :], xn_sb[s4][:, :], (f"xnsb{s4}",),
            (f"spxn_{j}",), f"spxn_{j}")
        ln_stats((f"xnsb{s4}",), xn_sb[s4][:, :], s4, 2, 3, 4, f"ln1_{j}")
        ts(base1[s2][:, :], xn_sb[s4][:, :], musd[s4][:, 2:3], ALU.add,
           (f"xnsb{s4}", f"musd{s4}_2", f"musd{s4}_3"), (f"base1_{s2}",),
           s2=musd[s4][:, 3:4], op1=ALU.mult, tag=f"base1_{j}")

    def a1_mid(j):
        s2, s4 = j % 2, j % 4
        for c in range(8):
            transp(pT[:, c * 128:(c + 1) * 128], base1[s2][:, c * 128:(c + 1) * 128],
                   (f"base1_{s2}",), ("pT",), tag=f"tb_{j}{c}")
        for c in range(8):
            cp(baseT[s2][:, c * 128:(c + 1) * 128], pT[:, c * 128:(c + 1) * 128],
               ("pT",), (f"baseT{s2}_{c}",), tag=f"tbe_{j}{c}")
        for c in range(8):
            lk = (f"baseT{s2}_{c}", "W_Wq", "W_Wk", "W_Wv", "Wr_s")
            lh = baseT[s2][:, c * 128:(c + 1) * 128]
            mm(B[0][:, :], lh, Wq_s[:, c * 1024:c * 1024 + 512], lk, ("B0",),
               start=(c == 0), stop=False, tag=f"q0_{j}{c}")
            mm(B[1][:, :], lh, Wq_s[:, c * 1024 + 512:(c + 1) * 1024], lk, ("B1",),
               start=(c == 0), stop=False, tag=f"q1_{j}{c}")
            mm(B[2][:, 0:256], lh, Wkv_s[:, c * 512:c * 512 + 256], lk, ("B2",),
               start=(c == 0), stop=False, tag=f"k_{j}{c}")
            mm(B[2][:, 256:512], lh, Wkv_s[:, c * 512 + 256:(c + 1) * 512], lk, ("B2",),
               start=False, stop=False, tag=f"v_{j}{c}")
            mm(B[3][:, 0:8], lh, Wr_s[:, c * 8:(c + 1) * 8], lk, ("B3",),
               start=(c == 0), stop=(c == 7), tag=f"r_{j}{c}")
        brd = ("brow_Wq", "brow_Wk", "brow_Wv", "c_onesrow")
        mm(B[0][:, :], C["onesrow"][:, :], brow[0:1, 0:512], brd, ("B0",),
           start=False, stop=True, tag=f"qb0_{j}")
        mm(B[1][:, :], C["onesrow"][:, :], brow[0:1, 512:1024], brd, ("B1",),
           start=False, stop=True, tag=f"qb1_{j}")
        mm(B[2][:, 0:256], C["onesrow"][:, :], brow[0:1, 1024:1280], brd, ("B2",),
           start=False, stop=True, tag=f"kb_{j}")
        mm(B[2][:, 256:512], C["onesrow"][:, :], brow[0:1, 1280:1536], brd, ("B2",),
           start=False, stop=True, tag=f"vb_{j}")
        ts(muS[s4][:, :], Sbc[:, :], musd[s4][:, 2:3], ALU.mult,
           ("Sbc", f"musd{s4}_2"), (f"muS{s4}",), tag=f"muS{j}")
        ts(muS[s4][:, :], muS[s4][:, :], -1.0, ALU.mult, (f"muS{s4}",), (f"muS{s4}",))
        stt(logits_all[:, j * 8:(j + 1) * 8], B[3][:, 0:8], musd[s4][:, 4:5],
            muS[s4][:, :], ALU.mult, ALU.add,
            ("B3", f"musd{s4}_4", f"muS{s4}"), (f"logit{j}", "logits_all"), tag=f"logf{j}")
        act(qtmp[s2][:, 0:512], B[0][:, :], AF.Copy, ("B0",), (f"qe0_{s2}",), tag=f"qe0_{j}")
        act(qtmp[s2][:, 512:1024], B[1][:, :], AF.Copy, ("B1",), (f"qe1_{s2}",), tag=f"qe1_{j}")
        act(qtmp[s2][:, 1024:1280], B[2][:, 0:256], AF.Copy, ("B2",), (f"ke_{s2}",), tag=f"ke_{j}")
        act(bass.AP(vsb[j % 4], 0, [vsb[j % 4][:, :].ap[0], [65, 4], [1, 64]]),
            B[2][:, 256:512].rearrange("p (g d) -> p g d", g=4), AF.Copy,
            ("B2", f"v{j % 4}"), (f"v{j % 4}",), tag=f"ve_{j}")
        cosb = C["cost"][:, j * 32:(j + 1) * 32].unsqueeze(1).broadcast_to([128, 16, 32])
        sinb = C["sint"][:, j * 32:(j + 1) * 32].unsqueeze(1).broadcast_to([128, 16, 32])
        cosb4 = C["cost"][:, j * 32:(j + 1) * 32].unsqueeze(1).broadcast_to([128, 4, 32])
        sinb4 = C["sint"][:, j * 32:(j + 1) * 32].unsqueeze(1).broadcast_to([128, 4, 32])
        qv = qtmp[s2][:, 0:1024].rearrange("p (h d) -> p h d", h=16)
        q1, q2 = qv[:, :, 0:32], qv[:, :, 32:64]
        ro = ropeq[s2][:, :].rearrange("p (h d) -> p h d", h=16)
        ta = tmpA[s2][:, :].rearrange("p (h d) -> p h d", h=16)
        tb = tmpB[s2][:, :].rearrange("p (h d) -> p h d", h=16)
        qks = (f"qe0_{s2}", f"qe1_{s2}")
        tt(ta, q1, cosb, ALU.mult, qks + ("c_cost",), (f"tmpA{s2}",), tag=f"r1_{j}")
        tt(tb, q2, sinb, ALU.mult, qks + ("c_sint",), (f"tmpB{s2}",), tag=f"r2_{j}")
        tt(ro[:, :, 0:32], ta, tb, ALU.subtract, (f"tmpA{s2}", f"tmpB{s2}"),
           (f"ropeq{s2}",), tag=f"r3_{j}")
        tt(ta, q2, cosb, ALU.mult, qks + ("c_cost",), (f"tmpA{s2}",), tag=f"r4_{j}")
        tt(tb, q1, sinb, ALU.mult, qks + ("c_sint",), (f"tmpB{s2}",), tag=f"r5_{j}")
        tt(ro[:, :, 32:64], ta, tb, ALU.add, (f"tmpA{s2}", f"tmpB{s2}"),
           (f"ropeq{s2}",), tag=f"r6_{j}")
        kv = qtmp[s2][:, 1024:1280].rearrange("p (h d) -> p h d", h=4)
        k1, k2 = kv[:, :, 0:32], kv[:, :, 32:64]
        rk = ropek[s2][:, :].rearrange("p (h d) -> p h d", h=4)
        ka = krA[s2][:, :].rearrange("p (h d) -> p h d", h=4)
        kb = krB[s2][:, :].rearrange("p (h d) -> p h d", h=4)
        kk_ = f"ke_{s2}"
        tt(ka, k1, cosb4, ALU.mult, (kk_, "c_cost"), (f"krA{s2}",), tag=f"k1_{j}")
        tt(kb, k2, sinb4, ALU.mult, (kk_, "c_sint"), (f"krB{s2}",), tag=f"k2_{j}")
        tt(rk[:, :, 0:32], ka, kb, ALU.subtract, (f"krA{s2}", f"krB{s2}"),
           (f"ropek{s2}",), tag=f"k3_{j}")
        tt(ka, k2, cosb4, ALU.mult, (kk_, "c_cost"), (f"krA{s2}",), tag=f"k4_{j}")
        tt(kb, k1, sinb4, ALU.mult, (kk_, "c_sint"), (f"krB{s2}",), tag=f"k5_{j}")
        tt(rk[:, :, 32:64], ka, kb, ALU.add, (f"krA{s2}", f"krB{s2}"),
           (f"ropek{s2}",), tag=f"k6_{j}")

    def a1_back(j):
        s2, s4 = j % 2, j % 4
        for r in range(8):
            hlo, hhi = Q_LO[r], Q_HI[r]
            transp(pT[0:64, r * 128:(r + 1) * 128], ropeq[s2][:, hlo * 64:hlo * 64 + 64],
                   (f"ropeq{s2}",), ("pT",), tp=(0, 0), tag=f"tq_{j}{r}l")
            transp(pT[64:128, r * 128:(r + 1) * 128], ropeq[s2][:, hhi * 64:hhi * 64 + 64],
                   (f"ropeq{s2}",), ("pT",), tp=(0, 64), tag=f"tq_{j}{r}h")
        for r in range(8):
            cp(qT[s2][:, r * 128:(r + 1) * 128], pT[:, r * 128:(r + 1) * 128],
               ("pT",), (f"qT{s2}_{r}",), tag=f"tqe_{j}{r}")
        for cch in range(2):
            transp(pT[:, cch * 128:(cch + 1) * 128],
                   ropek[s2][:, cch * 128:(cch + 1) * 128],
                   (f"ropek{s2}",), ("pT",), tag=f"tk_{j}{cch}")
        for cch in range(2):
            cp(kT[j % 4][:, cch * 128:(cch + 1) * 128], pT[:, cch * 128:(cch + 1) * 128],
               ("pT",), (f"kT{j % 4}",), tag=f"tke_{j}{cch}")
        kks = [kx for kx in (j - 2, j - 1, j) if kx >= 0]
        for g in range(4):
            asl = g % 2
            b0 = 64 * (g % 2)
            ccol = 128 * (g // 2)
            for ki, kk in enumerate(kks):
                dmask = j - kk
                lhsT = kT[kk % 4][b0:b0 + 64, ccol:ccol + 128]
                for h4 in range(4):
                    h = 4 * g + h4
                    rhs = qT[s2][b0:b0 + 64, Q_REG[h] * 128:(Q_REG[h] + 1) * 128]
                    mm(B[4 + ki][:, h4 * 128:(h4 + 1) * 128], lhsT, rhs,
                       (f"kT{kk % 4}", f"qT{s2}_{Q_REG[h]}"), (f"B{4 + ki}",),
                       start=(h4 == 0), stop=(dmask == 1 and h4 == 3),
                       tag=f"sc_{j}g{g}k{ki}h{h4}")
                if dmask != 1:
                    mrhs = C["m0"][:, :] if dmask == 0 else C["m2"][:, :]
                    mk = "c_m0" if dmask == 0 else "c_m2"
                    for h4 in range(4):
                        mm(B[4 + ki][:, h4 * 128:(h4 + 1) * 128], C["ident"][:, :], mrhs,
                           ("c_ident", mk), (f"B{4 + ki}",), start=False, stop=(h4 == 3),
                           tag=f"mk_{j}g{g}k{ki}h{h4}")
            for ki in range(len(kks)):
                act(attnT[asl][:, ki * 512:(ki + 1) * 512], B[4 + ki][:, :], AF.Exp,
                    (f"B{4 + ki}",), (f"attnT{asl}",), tag=f"exp_{j}g{g}k{ki}")
            for h4 in range(4):
                for ki, kk in enumerate(kks):
                    mm(B[3][:, 64 + h4 * 65: 64 + (h4 + 1) * 65],
                       attnT[asl][:, ki * 512 + h4 * 128: ki * 512 + (h4 + 1) * 128],
                       vsb[kk % 4][:, g * 65:(g + 1) * 65],
                       (f"attnT{asl}", f"v{kk % 4}"), ("B3",),
                       start=(h4 == 0 and ki == 0),
                       stop=(h4 == 3 and ki == len(kks) - 1), tag=f"pv_{j}g{g}h{h4}k{ki}")
            P.op("dve", lambda e, o=rrec[asl][:, 0:4],
                 i=bass.AP(B[3], 128, [B[3][:, :].ap[0], [65, 4], [1, 1]]):
                 e.reciprocal(o, i), ("B3",), (f"rrec{asl}",), f"rc_{j}g{g}")
            for h4 in range(4):
                h = 4 * g + h4
                act(attn_out[s2][:, h * 64:(h + 1) * 64],
                    B[3][:, 64 + h4 * 65: 64 + h4 * 65 + 64], AF.Copy,
                    ("B3", f"rrec{asl}"), (f"attn_out{s2}",),
                    scale=rrec[asl][:, h4:h4 + 1], tag=f"oev_{j}g{g}h{h4}")
        for c in range(8):
            transp(pT[:, c * 128:(c + 1) * 128], attn_out[s2][:, c * 128:(c + 1) * 128],
                   (f"attn_out{s2}",), ("pT",), tag=f"tao_{j}{c}")
        for c in range(8):
            cp(aoT[s2][:, c * 128:(c + 1) * 128], pT[:, c * 128:(c + 1) * 128],
               ("pT",), (f"aoT{s2}_{c}",), tag=f"taoe_{j}{c}")
        for c in range(8):
            lh = aoT[s2][:, c * 128:(c + 1) * 128]
            mm(B[0][:, :], lh, Wo_s[:, c * 1024:c * 1024 + 512],
               (f"aoT{s2}_{c}", "Wo_s"), ("B0",), start=(c == 0), stop=False, tag=f"wo0_{j}{c}")
            mm(B[1][:, :], lh, Wo_s[:, c * 1024 + 512:(c + 1) * 1024],
               (f"aoT{s2}_{c}", "Wo_s"), ("B1",), start=(c == 0), stop=False, tag=f"wo1_{j}{c}")
        mm(B[0][:, :], C["onesrow"][:, :], borow[0:1, 0:512], ("borow", "c_onesrow"),
           ("B0",), start=False, stop=True, tag=f"wob0_{j}")
        mm(B[1][:, :], C["onesrow"][:, :], borow[0:1, 512:1024], ("borow", "c_onesrow"),
           ("B1",), start=False, stop=True, tag=f"wob1_{j}")
        tt(x2sb[s2][:, 0:512], B[0][:, :], xn_sb[s4][:, 0:512], ALU.add,
           ("B0", f"xnsb{s4}", f"x2_{s2}a"), (f"x2_{s2}a",), tag=f"x2a_{j}")
        tt(x2sb[s2][:, 512:1024], B[1][:, :], xn_sb[s4][:, 512:1024], ALU.add,
           ("B1", f"xnsb{s4}", f"x2_{s2}b"), (f"x2_{s2}b",), tag=f"x2b_{j}")
        x2k = (f"x2_{s2}a", f"x2_{s2}b")
        ln_stats(x2k, x2sb[s2][:, :], s4, 5, 6, None, f"ln2_{j}")
        ts(base2[s2][:, :], x2sb[s2][:, :], musd[s4][:, 5:6], ALU.add,
           x2k + (f"musd{s4}_5", f"musd{s4}_6"), (f"base2_{s2}",),
           s2=musd[s4][:, 6:7], op1=ALU.mult, tag=f"b2_{j}")
        dma("qact", d_x2[j * 128:(j + 1) * 128, :], x2sb[s2][:, :], x2k,
            (f"spx2_{j}",), f"spx2_{j}")
        dma("qact", d_b2s[j * 128:(j + 1) * 128, :], base2[s2][:, :], (f"base2_{s2}",),
            (f"spb2_{j}",), f"spb2_{j}")

    P.alias("sq0", "stg_o")
    P.alias("sq1", "stg_o")
    a1_ldx(0)
    a1_ldx(1)
    a1_pre(0)
    a1_ldx(2)
    a1_pre(1)
    a1_mid(0)
    a1_ldx(3)
    a1_pre(2)
    a1_mid(1)
    for j in range(NT):
        a1_back(j)
        if j + 4 < NT:
            a1_ldx(j + 4)
        if j + 3 < NT:
            a1_pre(j + 3)
        if j + 2 < NT:
            a1_mid(j + 2)

    if debug:
        dma("qsp", dbg["xn"], d_xn, tuple(f"spxn_{j}" for j in range(NT)), (), "dbgxn")
        dma("qsp", dbg["logits"], logits_all[:, :],
            tuple(f"logit{j}" for j in range(NT)), (), "dbglog")
        dma("qsp", dbg["x2"], d_x2, tuple(f"spx2_{j}" for j in range(NT)), (), "dbgx2")

    LA = ("logits_all",)
    act(rt_f[:, :], logits_all[:, :], AF.Exp, LA, ("rt_f",), tag="rexp")
    red(zrec[:, :], rt_f[:, :].rearrange("p (a b) -> p a b", a=NT), ALU.add,
        ("rt_f",), ("zsum",), tag="rz")
    P.op("dve", lambda e: e.reciprocal(zrec[:, :], zrec[:, :]), ("zsum",), ("zrec",), "rzr")
    red(max1[:, :], logits_all[:, :].rearrange("p (a b) -> p a b", a=NT), ALU.max,
        LA, ("max1",), tag="rm1")
    tt(m1_all[:, :].rearrange("p (a b) -> p a b", a=NT),
       logits_all[:, :].rearrange("p (a b) -> p a b", a=NT),
       max1[:, :].unsqueeze(2).broadcast_to([128, NT, 8]), ALU.is_ge,
       LA + ("max1",), ("m1_all",), tag="rm1h")
    stt(rt_g[:, :], m1_all[:, :], NEG, logits_all[:, :], ALU.mult, ALU.add,
        ("m1_all",) + LA, ("rt_g",), tag="rmask")
    red(max2[:, :], rt_g[:, :].rearrange("p (a b) -> p a b", a=NT), ALU.max,
        ("rt_g",), ("max2",), tag="rm2")
    tt(m2_all[:, :].rearrange("p (a b) -> p a b", a=NT),
       rt_g[:, :].rearrange("p (a b) -> p a b", a=NT),
       max2[:, :].unsqueeze(2).broadcast_to([128, NT, 8]), ALU.is_ge,
       ("rt_g", "max2"), ("m2_all",), tag="rm2h")
    act(p1sb[:, :], max1[:, :], AF.Exp, ("max1",), ("p1e",), tag="rp1e")
    tt(p1sb[:, :], p1sb[:, :], zrec[:, :], ALU.mult, ("p1e", "zrec"), ("p1sb",), tag="rp1")
    act(p2sb[:, :], max2[:, :], AF.Exp, ("max2",), ("p2e",), tag="rp2e")
    tt(p2sb[:, :], p2sb[:, :], zrec[:, :], ALU.mult, ("p2e", "zrec"), ("p2sb",), tag="rp2")
    for snum, mall, mkey in ((0, m1_all, "m1_all"), (1, m2_all, "m2_all")):
        cb = B[4 + snum]
        for j in range(NT):
            mm(cb[:, j * 8:(j + 1) * 8], C["u128"][:, :], mall[:, j * 8:(j + 1) * 8],
               ("c_u128", mkey), (f"B{4 + snum}",), start=True, stop=True,
               tag=f"cs_{snum}_{j}")
            mm(B[6][0:1, snum * 256 + j * 8: snum * 256 + (j + 1) * 8],
               C["onescol"][:, :], mall[:, j * 8:(j + 1) * 8],
               ("c_onescol", mkey), ("B6",), start=True, stop=True,
               tag=f"ct_{snum}_{j}")
    crowA, crowB = ssb[1][0:1, 0:1024], ssb[2][0:1, 0:1024]
    memset(crowA, 0.0, ("carrow",), ("carrow",), "zcarA")
    memset(crowB, 0.0, ("carrow2",), ("carrow2",), "zcarB")
    cp(crowA[0:1, 512:1024], B[6][0:1, 0:512], ("B6",), ("carrow",), tag="ct_ev")
    cur, curk, oth, othk = crowA, "carrow", crowB, "carrow2"
    for st_ in range(6):
        d_ = 8 * (1 << st_)
        tt(oth[0:1, 512:1024], cur[0:1, 512:1024], cur[0:1, 512 - d_:1024 - d_],
           ALU.add, (curk,), (othk,), tag=f"pfx{st_}")
        cur, curk, oth, othk = oth, othk, cur, curk
    mm(B[3][:, 0:256], C["onesrowf"][:, :], cur[0:1, 504:760],
       (curk, "c_onesrowf"), ("B3",), start=True, stop=True, tag="cbc1")
    cp(rt_f[:, 0:256], B[3][:, 0:256], ("B3",), ("rt_f",), tag="cbc1e")
    mm(B[3][:, 0:256], C["onesrowf"][:, :], cur[0:1, 760:1016],
       (curk, "c_onesrowf"), ("B3",), start=True, stop=True, tag="cbc2")
    cp(rt_g[:, 0:256], B[3][:, 0:256], ("B3",), ("rt_g",), tag="cbc2e")
    for snum, mkey in ((0, "m1_all"), (1, "m2_all")):
        mall = m1_all if snum == 0 else m2_all
        crep = rt_f if snum == 0 else rt_g
        crepk = "rt_f" if snum == 0 else "rt_g"
        stt(rt_h[:, :], B[4 + snum][:, 0:NT * 8], 0.0, crep[:, 0:256],
            ALU.add, ALU.add, (f"B{4 + snum}", crepk), ("rt_h",), tag=f"pc_{snum}")
        tt(rt_h[:, :], rt_h[:, :], mall[:, :], ALU.mult,
           ("rt_h", mkey), ("rt_h",), tag=f"pm_{snum}")
        red(poscol[:, :], rt_h[:, :].rearrange("p (a b) -> p a b", a=NT), ALU.add,
            ("rt_h",), ("poscol",), tag=f"pr_{snum}")
        tt(rt_h[:, :].rearrange("p (a b) -> p a b", a=NT),
           mall[:, :].rearrange("p (a b) -> p a b", a=NT),
           C["iota8"][:, :].unsqueeze(1).broadcast_to([128, NT, 8]), ALU.mult,
           (mkey, "c_iota8"), ("rt_h",), tag=f"em_{snum}")
        red(ecol[:, :], rt_h[:, :].rearrange("p (a b) -> p a b", a=NT), ALU.add,
            ("rt_h",), ("ecol",), tag=f"er_{snum}")
        stt(idxf[:, :], ecol[:, :], float(CAP), poscol[:, :], ALU.mult, ALU.add,
            ("ecol", "poscol"), ("idxf",), tag=f"ix_{snum}")
        ts(idxf[:, :], idxf[:, :], -1.0, ALU.add, ("idxf",), ("idxf",), tag=f"ix1_{snum}")
        ts(bumpc[:, :], poscol[:, :], float(CAP) + 0.5, ALU.is_ge, ("poscol",),
           ("bumpc",), s2=1e6, op1=ALU.mult, tag=f"bp_{snum}")
        tt(idxf[:, :], idxf[:, :], bumpc[:, :], ALU.add, ("idxf", "bumpc"), ("idxf",),
           tag=f"ix2_{snum}")
        cp(idxu[snum][:, :], idxf[:, :], ("idxf",), (f"idxu{snum}",), tag=f"iu_{snum}")
    if debug:
        dma("qsp", dbg["idx1"], idxu[0][:, :], ("idxu0",), (), "dbgi1")
        dma("qsp", dbg["idx2"], idxu[1][:, :], ("idxu1",), (), "dbgi2")
        dma("qsp", dbg["p1"], p1sb[:, :], ("p1sb",), (), "dbgp1")
        dma("qsp", dbg["p2"], p2sb[:, :], ("p2sb",), (), "dbgp2")

    for j in range(NT):
        for snum in range(2):
            P.op("qindS", lambda e, i=C["tokids"][:, j:j + 1], ix=idxu[snum][:, j:j + 1]:
                 e.indirect_dma_start(out=d_tok, out_offset=bass.IndirectOffsetOnAxis(
                     ap=ix, axis=0), in_=i, in_offset=None,
                     bounds_check=breg, oob_is_err=False),
                 ("c_tokids", f"idxu{snum}", "d_tok_fill", "breg"), (f"tsc_{j}_{snum}",),
                 tag=f"scat{j}_{snum}")
    scat_done = tuple(f"tsc_{j}_{s}" for j in range(NT) for s in range(2))
    if debug:
        dma("qsp", dbg["tok"], d_tok, scat_done, (), "dbgtok")

    for c in range(8):
        P.alias(f"b2T0_{c}", "stg_o")
        P.alias(f"b2T1_{c}", "W_Wk", "W_Wv")
    for jf in range(32):
        P.alias(f"gT{jf}", "stage", "stg_o")
    P.alias("ssb0_0", "srow", "srow2")
    P.alias("ssb0_1", "srow", "srow2")
    P.alias("ssb1_0", "carrow")
    P.alias("ssb1_1", "carrow")
    P.alias("ssb2_0", "carrow2")
    P.alias("ssb2_1", "carrow2")
    P.alias("wf2b0", "Wo_s")
    P.alias("wf2b1", "Wo_s")
    P.alias("b2blk0", "base1_0")
    P.alias("b2blk1", "base1_1")
    for c in range(8):
        P.alias("b2blk2", f"baseT0_{c}")
        P.alias("b2blk3", f"baseT1_{c}")
    def ldb2blk(n):
        for i in range(4):
            jj = 4 * n + i
            dma("qsp", b2blk[i][:, :], d_b2s[jj * 128:(jj + 1) * 128, :],
                (f"spb2_{jj}",), (f"b2blk{i}",), f"ldb2_{jj}")

    ldb2blk(0)
    for n in range(8):
        for i in range(4):
            jj = 4 * n + i
            dma("qsp", x2blk[i][:, :], d_x2[jj * 128:(jj + 1) * 128, :],
                (f"spx2_{jj}",),
                (f"x2blk{i}",) + ((f"x2_{i}a", f"x2_{i}b")
                                  if i < 2 else (f"x_sb{i - 2}",)), f"ldx2_{jj}")
        bp = n % 2
        for i in range(4):
            for c in range(8):
                transp(pT[:, c * 128:(c + 1) * 128], b2blk[i][:, c * 128:(c + 1) * 128],
                       (f"b2blk{i}",), ("pT",), tag=f"tb2_{n}{i}{c}")
            for c in range(8):
                cp(b2Tb[bp][:, c * 512 + i * 128: c * 512 + (i + 1) * 128],
                   pT[:, c * 128:(c + 1) * 128], ("pT",), (f"b2T{bp}_{c}",),
                   tag=f"tb2e_{n}{i}{c}")
        if n + 1 < 8:
            ldb2blk(n + 1)
        for gi in range(8):
            dma("qsp", wf1buf[gi % 2][:, :],
                d_wf1p.rearrange("p (c f) -> p c f", c=8)[:, :, gi * 512:(gi + 1) * 512],
                (f"wf1b{gi % 2}", f"wf1p_{gi}"), (f"wf1b{gi % 2}",), f"ldwf1_{n}{gi}")
            for jq in range(4):
                jf = gi * 4 + jq
                bk = B[jf % 3]
                bkk = f"B{jf % 3}"
                for c in range(8):
                    mm(bk[:, :], wf1buf[gi % 2][:, c * 512 + jq * 128: c * 512 + (jq + 1) * 128],
                       b2Tb[bp][:, c * 512:(c + 1) * 512],
                       (f"wf1b{gi % 2}", f"b2T{bp}_{c}"), (bkk,),
                       start=(c == 0), stop=(c == 7), tag=f"f1_{n}{jf}{c}")
                rd = (bkk, "bf1col")
                act(gT[:, jf * 512:(jf + 1) * 512], bk[:, :], AF.Gelu, rd, (f"gT{jf}",),
                    bias=bf1col[:, jf:jf + 1], tag=f"gel_{n}{jf}")
        for half in range(2):
            for jg in range(8):
                dma(f"qp_wf2b{jg % 2}", wf2buf[jg % 2][:, :].rearrange("p (c f) -> p c f", c=4),
                    dW["Wf2"][jg * 512:(jg + 1) * 512, half * 512:(half + 1) * 512]
                    .rearrange("(c p) f -> p c f", p=128),
                    (f"wf2b{jg % 2}",), (f"wf2b{jg % 2}",), f"ldwf2_{n}{half}{jg}")
                for jq in range(4):
                    jf = jg * 4 + jq
                    for tsl in range(4):
                        mm(B[3 + tsl][:, :],
                           gT[:, jf * 512 + tsl * 128: jf * 512 + (tsl + 1) * 128],
                           wf2buf[jg % 2][:, jq * 512:(jq + 1) * 512],
                           (f"gT{jf}", f"wf2b{jg % 2}"), (f"B{3 + tsl}",),
                           start=(jf == 0), stop=False, tag=f"f2_{n}{half}{jf}{tsl}")
            for tsl in range(4):
                mm(B[3 + tsl][:, :], C["onesrow"][:, :],
                   bf2row[0:1, half * 512:(half + 1) * 512], ("bf2row", "c_onesrow"),
                   (f"B{3 + tsl}",), start=False, stop=True, tag=f"f2b_{n}{half}{tsl}")
            for tsl in range(4):
                jj = 4 * n + tsl
                tt(ssb[tsl][:, half * 512:(half + 1) * 512], B[3 + tsl][:, :],
                   x2blk[tsl][:, half * 512:(half + 1) * 512], ALU.add,
                   (f"B{3 + tsl}", f"x2blk{tsl}"), (f"ssb{tsl}_{half}",),
                   tag=f"sev_{n}{half}{tsl}")
                if half == 1:
                    dma("qact", d_s[jj * 128:(jj + 1) * 128, :], ssb[tsl][:, :],
                        (f"ssb{tsl}_0", f"ssb{tsl}_1"), (f"sps_{jj}",), f"sps_{jj}")
    if debug:
        dma("qsp", dbg["s"], d_s, tuple(f"sps_{jj}" for jj in range(NT)), (), "dbgs")

    P.alias("eib0", "W_Wk", "W_Wv", *[f"b2T1_{c}" for c in range(8)])
    P.alias("eib1", "W_Wk", "W_Wv", *[f"b2T1_{c}" for c in range(8)])
    for c in range(8):
        P.alias(f"eiT0_{c}", "W_Wq", "brow_Wq")
        P.alias(f"eiT1_{c}", "wf2b0", "wf2b1", "Wo_s")
    P.alias("wib1", "W_Wq")
    P.alias("wib0", *[f"b2T0_{c}" for c in range(8)])
    P.alias("eob0", "base1_0", "b2blk0")
    P.alias("eob1", "base1_1", "b2blk1")
    P.alias("eob2", "b2blk2", *[f"baseT0_{c}" for c in range(8)])
    P.alias("eob3", "b2blk3", *[f"baseT1_{c}" for c in range(8)])
    P.alias("wob0", "Wo_s", "wf2b0", "wf2b1")
    P.alias("wob1", "Wo_s", "wf2b0", "wf2b1")
    xn_done = tuple(f"spxn_{j}" for j in range(NT))

    def gblk(e_, i):
        ii = i % 2
        ti = 4 * e_ + i
        dma("qsp", tokcol[ii][:, :], d_tok[ti * 128:(ti + 1) * 128, :],
            (f"tokcol{ii}",) + (scat_done if e_ == 0 and i < 2 else ()),
            (f"tokcol{ii}",), f"ldtok_{e_}{i}")
        memset(eib[ii][:, :], 0.0, (f"eib{ii}",), (f"eib{ii}",), f"zei_{e_}{i}")
        P.op(f"qindE{ii}", lambda e, o=eib[ii][:, :], ix=tokcol[ii][:, :]:
             e.indirect_dma_start(out=o, out_offset=None, in_=d_xn,
                 in_offset=bass.IndirectOffsetOnAxis(ap=ix, axis=0),
                 bounds_check=breg, oob_is_err=False),
             (f"tokcol{ii}", f"eib{ii}", "breg") + (xn_done if e_ == 0 and i < 2 else ()),
             (f"eib{ii}",), f"gei_{e_}{i}")

    def teib(e_, i):
        ii = i % 2
        for c in range(8):
            transp(pT[:, c * 128:(c + 1) * 128], eib[ii][:, c * 128:(c + 1) * 128],
                   (f"eib{ii}",), ("pT",), tag=f"tei_{e_}{i}{c}")
        for c in range(8):
            cp(eiTb[e_ % 2][:, c * 512 + i * 128: c * 512 + (i + 1) * 128],
               pT[:, c * 128:(c + 1) * 128], ("pT",), (f"eiT{e_ % 2}_{c}",),
               tag=f"teie_{e_}{i}{c}")

    gblk(0, 0)
    gblk(0, 1)
    for e_ in range(NE):
        teib(e_, 0)
        teib(e_, 1)
        gblk(e_, 2)
        gblk(e_, 3)
        teib(e_, 2)
        teib(e_, 3)
        if e_ + 1 < NE:
            gblk(e_ + 1, 0)
            gblk(e_ + 1, 1)
        for gi in range(8):
            dma(f"qp_wib{gi % 2}", wib[gi % 2][:, :].rearrange("p (c f) -> p c f", c=8),
                dW["wi2"][e_ * D:(e_ + 1) * D, gi * 512:(gi + 1) * 512]
                .rearrange("(c p) f -> p c f", p=128),
                (f"wib{gi % 2}",), (f"wib{gi % 2}",), f"ldwi_{e_}{gi}")
            for jq in range(4):
                jf = gi * 4 + jq
                bk = B[jf % 3]
                bkk = f"B{jf % 3}"
                for c in range(8):
                    mm(bk[:, :], wib[gi % 2][:, c * 512 + jq * 128: c * 512 + (jq + 1) * 128],
                       eiTb[e_ % 2][:, c * 512:(c + 1) * 512],
                       (f"wib{gi % 2}", f"eiT{e_ % 2}_{c}"), (bkk,),
                       start=(c == 0), stop=(c == 7), tag=f"wi_{e_}{jf}{c}")
                act(gT[:, jf * 512:(jf + 1) * 512], bk[:, :], AF.Gelu_apprx_tanh,
                    (bkk,), (f"gT{jf}",), tag=f"gt_{e_}{jf}")
        for half in range(2):
            for jg in range(8):
                dma(f"qp_wob{jg % 2}", wob[jg % 2][:, :].rearrange("p (c f) -> p c f", c=4),
                    dW["wo2"][(e_ * DFF + jg * 512):(e_ * DFF + (jg + 1) * 512),
                              half * 512:(half + 1) * 512]
                    .rearrange("(c p) f -> p c f", p=128),
                    (f"wob{jg % 2}",), (f"wob{jg % 2}",), f"ldwo_{e_}{half}{jg}")
                for jq in range(4):
                    jf = jg * 4 + jq
                    for cs in range(4):
                        mm(B[3 + cs][:, :],
                           gT[:, jf * 512 + cs * 128: jf * 512 + (cs + 1) * 128],
                           wob[jg % 2][:, jq * 512:(jq + 1) * 512],
                           (f"gT{jf}", f"wob{jg % 2}"), (f"B{3 + cs}",),
                           start=(jf == 0), stop=(jf == 31), tag=f"woM_{e_}{half}{jf}{cs}")
            for cs in range(4):
                act(eob4[cs][:, half * 512:(half + 1) * 512],
                    B[3 + cs][:, :], AF.Copy, (f"B{3 + cs}",), (f"eob{cs}",),
                    tag=f"eoe_{e_}{half}{cs}")
        for cs in range(4):
            dma("qact", d_eo[(4 * e_ + cs) * 128:(4 * e_ + cs + 1) * 128, :],
                eob4[cs][:, :], (f"eob{cs}",),
                (f"eost_{e_}_{cs}",), f"steo_{e_}{cs}")
    eo_done = tuple(f"eost_{e_}_{cs}" for e_ in range(NE) for cs in range(4))
    if debug:
        dma("qsp", dbg["eo"], d_eo, eo_done, (), "dbgeo")

    for k in range(4):
        P.alias(f"gth0_{k}", "wf2b0", "wf2b1", "Wo_s",
                *[f"eiT1_{c}" for c in range(8)])
        P.alias(f"gth1_{k}", "wob0", "wob1", "Wo_s")
    P.alias("moes0", "ssb0_0", "ssb0_1")
    P.alias("moes1", "ssb1_0", "ssb1_1", "carrow")
    P.alias("gsil0", "ssb2_0", "ssb2_1", "carrow2")
    P.alias("gsil1", "ssb3_0", "ssb3_1")
    P.alias("sfin0", "x2blk0", "x2_0a", "x2_0b")
    P.alias("sfin1", "x2blk1", "x2_1a", "x2_1b")
    P.alias("x_sb0", "x2blk2")
    P.alias("x_sb1", "x2blk3")
    for jg in range(NT // 2):
        h = jg % 2
        qch = "qindA" if h == 0 else "qindB"
        gkeys = tuple(f"gth{h}_{k}" for k in range(4))
        for i, j in enumerate((2 * jg, 2 * jg + 1)):
            sl = j % 2
            dma("qsp", x_sb[sl][:, :], dx[j * 128:(j + 1) * 128, :],
                (), (f"x_sb{sl}",), f"ldxf_{j}")
            dma("qsp", sfin[sl][:, :], d_s[j * 128:(j + 1) * 128, :],
                (f"sps_{j}",), (f"sfin{sl}",), f"ldsf_{j}")
            for snum in range(2):
                gk = f"gth{h}_{2 * i + snum}"
                memset(gset[h][2 * i + snum][:, :], 0.0, (gk,), (gk,),
                       f"zg_{j}{snum}")
                P.op(qch, lambda e, o=gset[h][2 * i + snum][:, :],
                     ix=idxu[snum][:, j:j + 1]:
                     e.indirect_dma_start(out=o, out_offset=None, in_=d_eo,
                         in_offset=bass.IndirectOffsetOnAxis(ap=ix, axis=0),
                         bounds_check=breg, oob_is_err=False),
                     (f"idxu{snum}", gk, "breg") + eo_done, (gk,),
                     f"gath{j}_{snum}")
        for i, j in enumerate((2 * jg, 2 * jg + 1)):
            sl = j % 2
            ts(moes[sl][:, :], gset[h][2 * i][:, :], p1sb[:, j:j + 1], ALU.mult,
               gkeys + ("p1sb",), (f"moes{sl}",), tag=f"cmb1_{j}")
            stt(moes[sl][:, :], gset[h][2 * i + 1][:, :], p2sb[:, j:j + 1],
                moes[sl][:, :], ALU.mult, ALU.add,
                (f"gth{h}_{2 * i + 1}", "p2sb", f"moes{sl}"), (f"moes{sl}",),
                tag=f"cmb2_{j}")
            act(gsil[sl][:, :], moes[sl][:, :], AF.Silu, (f"moes{sl}",),
                (f"gsil{sl}",), tag=f"sil_{j}")
            tt(sfin[sl][:, :], sfin[sl][:, :], gsil[sl][:, :], ALU.mult,
               (f"sfin{sl}", f"gsil{sl}"), (f"sfin{sl}",), tag=f"og_{j}")
            tt(sfin[sl][:, :], sfin[sl][:, :], x_sb[sl][:, :], ALU.add,
               (f"sfin{sl}", f"x_sb{sl}"), (f"sfin{sl}",), tag=f"ox_{j}")
            dma("qact", dout[j * 128:(j + 1) * 128, :], sfin[sl][:, :],
                (f"sfin{sl}",), (f"outst_{j}",), f"stout_{j}")

    P.finalize()
    chans = sorted({o.ch for eng in P.stream for o in P.stream[eng]} | set(ENGCH))
    sems = {chn: ctx.enter_context(nc.semaphore("sem_" + chn)) for chn in chans}
    with nc.Block() as block:
        @block.tensor
        def _(e):
            P.emit_engine("pe", e, sems)

        @block.scalar
        def _(e):
            P.emit_engine("act", e, sems)

        @block.vector
        def _(e):
            P.emit_engine("dve", e, sems)

        @block.gpsimd
        def _(e):
            P.emit_engine("pool", e, sems)

        @block.sync
        def _(e):
            P.emit_engine("sp", e, sems)
            for chn, v in sorted(P.final_counts.items()):
                e.wait_ge(sems[chn], v)
    ctx.close()
    return nc


def make_in_maps(inputs):
    x = np.asarray(inputs["L0_attn_x"], np.float32)
    consts = host_consts()
    base = {
        "Wq": np.asarray(inputs["L0_attn_Wq"], np.float32),
        "bq": np.asarray(inputs["L0_attn_bq"], np.float32).reshape(1, -1),
        "Wk": np.asarray(inputs["L0_attn_Wk"], np.float32),
        "bk": np.asarray(inputs["L0_attn_bk"], np.float32).reshape(1, -1),
        "Wv": np.asarray(inputs["L0_attn_Wv"], np.float32),
        "bv": np.asarray(inputs["L0_attn_bv"], np.float32).reshape(1, -1),
        "Wo": np.asarray(inputs["L0_attn_Wo"], np.float32),
        "bo": np.asarray(inputs["L0_attn_bo"], np.float32).reshape(1, -1),
        "g1": np.ascontiguousarray(np.asarray(inputs["L0_attn_g1"], np.float32).reshape(8, 128).T),
        "b1": np.ascontiguousarray(np.asarray(inputs["L0_attn_b1"], np.float32).reshape(8, 128).T),
        "g2": np.ascontiguousarray(np.asarray(inputs["L0_attn_g2"], np.float32).reshape(8, 128).T),
        "b2": np.ascontiguousarray(np.asarray(inputs["L0_attn_b2"], np.float32).reshape(8, 128).T),
        "Wf1": np.asarray(inputs["L0_attn_Wf1"], np.float32),
        "bf1c": np.ascontiguousarray(np.asarray(inputs["L0_attn_bf1"], np.float32).reshape(32, 128).T),
        "Wf2": np.asarray(inputs["L0_attn_Wf2"], np.float32),
        "bf2": np.asarray(inputs["L0_attn_bf2"], np.float32).reshape(1, -1),
        "Wr": np.asarray(inputs["L0_ffn_w_router"], np.float32),
        "wi2": np.asarray(inputs["L0_ffn_wi"], np.float32).reshape(NE * D, DFF),
        "wo2": np.asarray(inputs["L0_ffn_wo"], np.float32).reshape(NE * DFF, D),
    }
    for nm, v in consts.items():
        base["dc_" + nm] = v
    in_maps = []
    for b in range(8):
        m = dict(base)
        m["x"] = np.ascontiguousarray(x[b])
        in_maps.append(m)
    return in_maps


def kernel(**inputs):
    in_maps = make_in_maps(inputs)
    nc = build(debug=False)
    res = run_bass_kernel_spmd(nc, in_maps, core_ids=list(range(8)))
    return np.stack([np.asarray(r["out"], np.float32) for r in res.results], axis=0)


# version 15 (final)
# speedup vs baseline: 1.0470x; 1.0470x over previous
from contextlib import ExitStack

import numpy as np
import ml_dtypes

import concourse.bass as bass
import concourse.mybir as mybir
from concourse.bass_utils import run_bass_kernel_spmd

F32 = mybir.dt.float32
BF16 = mybir.dt.bfloat16
U32 = mybir.dt.uint32
AF = mybir.ActivationFunctionType
ALU = mybir.AluOpType
AX = mybir.AxisListType

T = 4096
D = 1024
NT = 32
H = 16
HD = 64
DFF = 4096
NE = 8
CAP = 512
EPS = 1e-5
NEG = -1e9
OOBTOK = 1000000

Q_LO = [0, 1, 2, 3, 8, 9, 10, 11]
Q_HI = [4, 5, 6, 7, 12, 13, 14, 15]
Q_REG = {h: r for r, h in enumerate(Q_LO)}
Q_REG.update({h: r for r, h in enumerate(Q_HI)})

ENGCH = {"pe": "pe", "act": "act", "dve": "dve", "pool": "pool", "sp": "sp"}


def issuer_of(ch):
    if ch in ENGCH:
        return ch
    if ch.startswith("qs_"):
        return "sp"
    if ch.startswith("qa_"):
        return "act"
    return "pool"


FINAL_WAIT = {"qp_init"}


class Op:
    __slots__ = ("ch", "emit", "deps", "needs_inc", "val", "tag")

    def __init__(self, ch, emit, tag):
        self.ch = ch
        self.emit = emit
        self.deps = []
        self.needs_inc = False
        self.val = 0
        self.tag = tag


class Prog:
    def __init__(self):
        self.stream = {e: [] for e in ("pe", "act", "dve", "pool", "sp")}
        self.res = {}

    def op(self, ch, emit, reads=(), writes=(), tag=""):
        o = Op(ch, emit, tag)
        dep = {}
        for k in reads:
            e = self.res.get(k)
            if e is not None and e[0] is not None:
                dep[id(e[0])] = e[0]
        for k in writes:
            e = self.res.get(k)
            if e is not None:
                if e[0] is not None:
                    dep[id(e[0])] = e[0]
                for r in e[1]:
                    dep[id(r)] = r
        for d in dep.values():
            if d is o:
                continue
            if d.ch == "pe" and ch == "pe":
                continue
            o.deps.append(d)
        for k in reads:
            self.res.setdefault(k, [None, []])[1].append(o)
        for k in writes:
            self.res[k] = [o, []]
        self.stream[issuer_of(ch)].append(o)
        return o

    def alias(self, dst, *srcs):
        ent = self.res.setdefault(dst, [None, []])
        for s in srcs:
            e = self.res.get(s)
            if e is not None:
                if e[0] is not None:
                    ent[1].append(e[0])
                ent[1].extend(e[1])

    def finalize(self):
        for eng in self.stream:
            for o in self.stream[eng]:
                for d in o.deps:
                    d.needs_inc = True
        counts = {}
        for eng in self.stream:
            for o in self.stream[eng]:
                if o.ch.startswith("q"):
                    o.needs_inc = True
                if o.needs_inc:
                    counts[o.ch] = counts.get(o.ch, 0) + (16 if o.ch.startswith("q") else 1)
                    o.val = counts[o.ch]
        self.final_counts = counts

    def emit_engine(self, eng, engobj, sems):
        clock = {}
        for o in self.stream[eng]:
            for d in o.deps:
                v = self.final_counts[d.ch] if d.ch in FINAL_WAIT else d.val
                if clock.get(d.ch, 0) < v:
                    engobj.wait_ge(sems[d.ch], v)
                    clock[d.ch] = v
            try:
                inst = o.emit(engobj)
            except Exception:
                print("EMIT FAILED at op:", o.tag, o.ch)
                raise
            if o.needs_inc:
                inst.then_inc(sems[o.ch], 16 if o.ch.startswith("q") else 1)


CONST_SPECS = {
    "ident": ([128, 128], BF16), "u128": ([128, 128], BF16),
    "onesrow": ([1, 128], BF16), "onesrowf": ([1, 128], F32),
    "onescol": ([128, 1], BF16),
    "m0": ([128, 128], BF16), "m2": ([128, 128], BF16),
    "cost": ([128, NT * 32], BF16), "sint": ([128, NT * 32], BF16),
    "iota8": ([128, 8], F32), "tokids": ([128, NT], U32),
    "epsb": ([128, 1], F32),
}


def host_consts():
    bf = ml_dtypes.bfloat16
    c = {}
    c["ident"] = np.eye(128, dtype=bf)
    c["u128"] = np.triu(np.ones((128, 128), np.float32)).astype(bf)
    c["onesrow"] = np.ones((1, 128), bf)
    c["onesrowf"] = np.ones((1, 128), np.float32)
    c["onescol"] = np.ones((128, 1), bf)
    kk = np.arange(128)[:, None]
    qq = np.arange(128)[None, :]
    c["m0"] = np.where(kk <= qq, 0.0, NEG).astype(np.float32).astype(bf)
    c["m2"] = np.where(qq < kk, 0.0, NEG).astype(np.float32).astype(bf)
    half = HD // 2
    theta = 1.0 / 10000.0 ** (np.arange(half, dtype=np.float32) / half)
    fr = np.outer(np.arange(T, dtype=np.float32), theta).astype(np.float32)
    cosv = np.cos(fr).astype(np.float32).reshape(NT, 128, half).transpose(1, 0, 2)
    sinv = np.sin(fr).astype(np.float32).reshape(NT, 128, half).transpose(1, 0, 2)
    c["cost"] = np.ascontiguousarray(cosv.reshape(128, NT * half)).astype(bf)
    c["sint"] = np.ascontiguousarray(sinv.reshape(128, NT * half)).astype(bf)
    c["iota8"] = np.tile(np.arange(8, dtype=np.float32), (128, 1))
    c["epsb"] = np.full((128, 1), EPS, np.float32)
    c["tokids"] = np.ascontiguousarray(
        (np.arange(NT, dtype=np.uint32)[None, :] * 128
         + np.arange(128, dtype=np.uint32)[:, None]))
    return c


def build(debug=False):
    nc = bass.Bass("TRN2", target_bir_lowering=False)
    P = Prog()

    dx = nc.dram_tensor("x", [T, D], F32, kind="ExternalInput").ap()
    dW = {}
    for nm, shp in [("Wq", [D, D]), ("bq", [1, D]), ("Wk", [D, 256]), ("bk", [1, 256]),
                    ("Wv", [D, 256]), ("bv", [1, 256]), ("Wo", [D, D]), ("bo", [1, D]),
                    ("g1", [128, 8]), ("b1", [128, 8]), ("g2", [128, 8]), ("b2", [128, 8]),
                    ("Wf1", [D, DFF]), ("bf1c", [128, 32]), ("Wf2", [DFF, D]), ("bf2", [1, D]),
                    ("Wr", [D, NE]), ("wi2", [NE * D, DFF]), ("wo2", [NE * DFF, D])]:
        dW[nm] = nc.dram_tensor(nm, shp, F32, kind="ExternalInput").ap()
    dC = {nm: nc.dram_tensor("dc_" + nm, shp, dt, kind="ExternalInput").ap()
          for nm, (shp, dt) in CONST_SPECS.items()}
    dout = nc.dram_tensor("out", [T, D], F32, kind="ExternalOutput").ap()
    d_xn = nc.dram_tensor("scr_xn", [T, D], BF16, kind="Internal").ap()
    d_x2 = nc.dram_tensor("scr_x2", [T, D], F32, kind="Internal").ap()
    d_b2s = nc.dram_tensor("scr_base2", [T, D], BF16, kind="Internal").ap()
    d_s = nc.dram_tensor("scr_s", [T, D], F32, kind="Internal").ap()
    d_wf1p = nc.dram_tensor("scr_wf1p", [128, 8 * DFF], BF16, kind="Internal").ap()
    d_tok = nc.dram_tensor("scr_tok", [NE * CAP, 1], U32, kind="Internal").ap()
    d_eo = nc.dram_tensor("scr_eo", [NE * CAP, D], BF16, kind="Internal").ap()
    dbg = {}
    if debug:
        for nm, shp, dt in [("Wq", [128, 8 * 1024], BF16), ("Wo", [128, 8 * 1024], BF16),
                            ("Wkv", [128, 8 * 512], BF16), ("brow", [1, 1536], BF16),
                            ("qtmp", [128, D + 256], BF16), ("ropeq", [128, D], BF16),
                            ("ropek", [128, 256], BF16), ("qT", [128, 8 * 128], BF16),
                            ("kT", [128, 256], BF16), ("vsb", [128, 260], BF16),
                            ("attn_out", [128, D], BF16), ("attnT", [128, 1536], BF16),
                            ("xn", [T, D], BF16), ("logits", [128, NT * 8], F32),
                            ("x2", [T, D], F32), ("s", [T, D], F32),
                            ("idx1", [128, NT], U32), ("idx2", [128, NT], U32),
                            ("p1", [128, NT], F32), ("p2", [128, NT], F32),
                            ("tok", [NE * CAP, 1], U32), ("eo", [NE * CAP, D], BF16)]:
            dbg[nm] = nc.dram_tensor("dbg_" + nm, shp, dt, kind="ExternalOutput").ap()

    ctx = ExitStack()
    sbt = lambda name, shape, dt: ctx.enter_context(nc.sbuf_tensor(name, shape, dt))
    pst = lambda name, shape, dt: ctx.enter_context(nc.psum_tensor(name, shape, dt))

    C = {nm: sbt("c_" + nm, *CONST_SPECS[nm]) for nm in CONST_SPECS}
    gb = {nm: sbt("v_" + nm, [128, 8], F32) for nm in ("g1", "b1", "g2", "b2")}

    gT = sbt("gT", [128, 32 * 512], BF16)
    stage = gT[:, :].bitcast(F32)
    Wq_s = sbt("Wq_s", [128, 8 * 1024], BF16)
    Wo_s = sbt("Wo_s", [128, 8 * 1024], BF16)
    Wkv_s = sbt("Wkv_s", [128, 8 * 512], BF16)
    b2T = sbt("b2T", [128, 8 * 512], BF16)
    Wr_s = sbt("Wr_s", [128, 8 * 8], BF16)
    wf1buf = [sbt(f"wf1b{i}", [128, 8 * 512], BF16) for i in range(2)]
    ssb = [sbt(f"ssb{i}", [128, D], F32) for i in range(4)]

    x_sb = [sbt(f"x_sb{i}", [128, D], F32) for i in range(2)]
    x2sb = [sbt(f"x2sb{i}", [128, D], F32) for i in range(2)]
    xn_sb = [sbt(f"xn_{i}", [128, D], BF16) for i in range(4)]
    stat = [sbt(f"stat{i}", [128, 24], F32) for i in range(4)]
    musd = [sbt(f"musd{i}", [128, 8], F32) for i in range(4)]
    base1 = [sbt(f"base1_{i}", [128, D], BF16) for i in range(2)]
    baseT = [sbt(f"baseT{i}", [128, D], BF16) for i in range(2)]
    qtmp = [sbt(f"qtmp{i}", [128, D + 256], BF16) for i in range(2)]
    ropeq = [sbt(f"ropeq{i}", [128, D], BF16) for i in range(2)]
    ropek = [sbt(f"ropek{i}", [128, 256], BF16) for i in range(2)]
    tmpA = [sbt(f"tmpA{i}", [128, 512], BF16) for i in range(2)]
    tmpB = [sbt(f"tmpB{i}", [128, 512], BF16) for i in range(2)]
    krA = [sbt(f"krA{i}", [128, 128], BF16) for i in range(2)]
    krB = [sbt(f"krB{i}", [128, 128], BF16) for i in range(2)]
    qT = [sbt(f"qT{i}", [128, 8 * 128], BF16) for i in range(2)]
    kT = [sbt(f"kT{i}", [128, 256], BF16) for i in range(4)]
    vsb = [sbt(f"v{i}", [128, 4 * 65], BF16) for i in range(4)]
    attnT = [sbt(f"attnT{i}", [128, 3 * 512], BF16) for i in range(2)]
    rrec = [sbt(f"rrec{i}", [128, 4], F32) for i in range(2)]
    attn_out = [sbt(f"attn_out{i}", [128, D], BF16) for i in range(2)]
    aoT = [sbt(f"aoT{i}", [128, D], BF16) for i in range(2)]
    base2 = [sbt(f"base2_{i}", [128, D], BF16) for i in range(2)]
    logits_all = sbt("logits_all", [128, NT * 8], F32)
    muS = [sbt(f"muS{i}", [128, 8], F32) for i in range(4)]

    brow = sbt("brow", [1, 1536], BF16)
    borow = sbt("borow", [1, D], BF16)
    bf1col = sbt("bf1col", [128, 32], F32)
    bf2row = sbt("bf2row", [1, D], BF16)
    Sbc = sbt("Sbc", [128, 8], F32)
    srow = ssb[0][0:1, 0:512]
    srow2 = ssb[0][0:1, 512:520]
    bf1tmp = sbt("bf1tmp", [128, 32], F32)
    tokfill = sbt("tokfill", [32, 128], U32)
    tokcol = [sbt(f"tokcol{i}", [128, 1], U32) for i in range(2)]

    m1_all = sbt("m1_all", [128, NT * 8], BF16)
    m2_all = sbt("m2_all", [128, NT * 8], BF16)
    rt_f = sbt("rt_f", [128, NT * 8], F32)
    rt_g = sbt("rt_g", [128, NT * 8], F32)
    rt_h = sbt("rt_h", [128, NT * 8], F32)
    max1 = sbt("max1", [128, NT], F32)
    max2 = sbt("max2", [128, NT], F32)
    zrec = sbt("zrec", [128, NT], F32)
    ecol = sbt("ecol", [128, NT], F32)
    poscol = sbt("poscol", [128, NT], F32)
    bumpc = sbt("bumpc", [128, NT], F32)
    idxf = sbt("idxf", [128, NT], F32)
    p1sb = sbt("p1sb", [128, NT], F32)
    p2sb = sbt("p2sb", [128, NT], F32)
    idxu = [sbt(f"idxu{i}", [128, NT], U32) for i in range(2)]

    stg_o = b2T
    wf2buf = [Wo_s[:, 0:2048], Wo_s[:, 2048:4096]]
    b2blk = [Wkv_s[:, i * 1024:(i + 1) * 1024] for i in range(4)]
    x2blk = [x2sb[0], x2sb[1], x_sb[0], x_sb[1]]
    eib = [Wkv_s[:, 0:1024], Wkv_s[:, 1024:2048]]
    eiT = Wq_s[:, 0:4096]
    wib = [b2T, Wq_s[:, 4096:8192]]
    eob4 = [base1[0], base1[1], baseT[0], baseT[1]]
    wob = [Wo_s[:, 4096:6144], Wo_s[:, 6144:8192]]
    gset = [[Wo_s[:, (4 * h + k) * 1024:(4 * h + k + 1) * 1024]
             for k in range(4)] for h in range(2)]
    moes = [ssb[0], ssb[1]]
    gsil = [ssb[2], ssb[3]]
    sfin = x2sb

    pT = pst("pT", [128, 1024], BF16)
    B = [pst(f"B{i}", [128, 512], F32) for i in range(7)]

    def dma(q, out, in_, reads, writes, tag=""):
        return P.op(q, lambda e, o=out, i=in_: e.dma_start(out=o, in_=i), reads, writes, tag)

    def mm(out, lhsT, rhs, reads, writes, start, stop, tag=""):
        P.op("pe", lambda e, o=out, l=lhsT, r=rhs, s1=start, s2=stop:
             e.matmul(o, l, r, start=s1, stop=s2, skip_group_check=True),
             reads, writes, tag)

    def transp(out, in_, reads, writes, tp=None, tag=""):
        P.op("pe", lambda e, o=out, i=in_, t=tp:
             e.transpose(o, i, C["ident"][:, :], tile_position=t),
             tuple(reads) + ("c_ident",), writes, tag)

    def act(out, in_, func, reads, writes, bias=0.0, scale=1.0, accum=None, tag=""):
        P.op("act", lambda e, o=out, i=in_, f=func, b=bias, s=scale, a=accum:
             e.activation(o, i, f, bias=b, scale=s, accum_out=a), reads, writes, tag)

    def tt(out, in0, in1, op, reads, writes, tag="", ch="dve"):
        P.op(ch, lambda e, o=out, a=in0, b=in1, p=op: e.tensor_tensor(o, a, b, p),
             reads, writes, tag)

    def ts(out, in0, s1, op0, reads, writes, s2=None, op1=None, tag="", ch="dve"):
        if op1 is None:
            P.op(ch, lambda e, o=out, i=in0, a=s1, p=op0:
                 e.tensor_scalar(o, i, a, None, p), reads, writes, tag)
        else:
            P.op(ch, lambda e, o=out, i=in0, a=s1, b=s2, p=op0, p2=op1:
                 e.tensor_scalar(o, i, a, b, p, p2), reads, writes, tag)

    def stt(out, in0, scal, in1, op0, op1, reads, writes, tag="", ch="dve"):
        P.op(ch, lambda e, o=out, i=in0, s=scal, j=in1, p=op0, p2=op1:
             e.scalar_tensor_tensor(o, i, s, j, p, p2), reads, writes, tag)

    def red(out, in_, op, reads, writes, tag="", ch="dve"):
        P.op(ch, lambda e, o=out, i=in_, p=op: e.tensor_reduce(o, i, AX.X, p),
             reads, writes, tag)

    def cp(out, in_, reads, writes, tag="", ch="dve"):
        P.op(ch, lambda e, o=out, i=in_: e.tensor_copy(o, i), reads, writes, tag)

    def evac(out, in_, reads, writes, c, tag=""):
        if c % 2 == 0:
            act(out, in_, AF.Copy, reads, writes, tag=tag)
        else:
            cp(out, in_, reads, writes, tag=tag, ch="dve")

    def memset(buf_ap, val, reads, writes, tag=""):
        P.op("pool", lambda e, o=buf_ap, v=val: e.memset(o, v), reads, writes, tag)

    def bn(out, in_, reads, writes, tag=""):
        P.op("dve", lambda e, o=out, i=in_: e.bn_stats(o, i), reads, writes, tag)

    def bnag(out, in_, reads, writes, tag=""):
        P.op("dve", lambda e, o=out, i=in_: e.bn_aggr(o, i), reads, writes, tag)

    breg = nc.gpsimd.alloc_register("bnd")
    P.op("pool", lambda e: e.reg_mov(breg, NE * CAP - 1), (), ("breg",), "bndmov")
    for nm in CONST_SPECS:
        dma("qsp", C[nm][:, :], dC[nm], (), ("c_" + nm,), f"ld_{nm}")
    for nm in gb:
        dma("qsp", gb[nm][:, :], dW[nm], (), ("v_" + nm,))
    dma("qp_init", Wo_s[:, :].rearrange("p (c n) -> p c n", c=8),
        dW["Wo"].rearrange("(c p) n -> p c n", p=128), (), ("Wo_s",))
    dma("qp_init", Wr_s[:, :].rearrange("p (c n) -> p c n", c=8),
        dW["Wr"].rearrange("(c p) n -> p c n", p=128), (), ("Wr_s",))
    dma("qp_init", borow[:, :], dW["bo"], (), ("borow",))
    dma("qp_init", bf2row[:, :], dW["bf2"], (), ("bf2row",))
    for i in range(4):
        memset(bass.AP(vsb[i], 64, [vsb[i][:, :].ap[0], [65, 4], [1, 1]]), 1.0,
               (), (f"v{i}",), f"vones{i}")
    memset(tokfill[:, :], OOBTOK, (), ("tokfill",), "tokfill")
    dma("qp_init", d_tok.rearrange("(a p) o -> a (p o)", p=128), tokfill[:, :],
        ("tokfill",), ("d_tok_fill",), "tokinit")

    def prep_weight(wname, bname, ncols, out_sb, col_of, brow_off, scale=None):
        dma("qsp", stage[:, 0:8 * ncols].rearrange("p (c n) -> p c n", c=8),
            dW[wname].rearrange("(c p) n -> p c n", p=128),
            ("stage",), ("stage",), f"stg_{wname}")
        for c in range(8):
            ts(out_sb[:, col_of(c):col_of(c) + ncols],
               stage[:, c * ncols:(c + 1) * ncols], gb["g1"][:, c:c + 1], ALU.mult,
               ("stage", "v_g1"), (f"W_{wname}",), s2=scale,
               op1=(None if scale is None else ALU.mult), tag=f"scl_{wname}{c}")
        nch = (ncols + 511) // 512
        for jc in range(nch):
            w = min(512, ncols - jc * 512)
            for c in range(8):
                mm(B[0][0:1, 0:w], gb["b1"][:, c:c + 1],
                   stage[:, c * ncols + jc * 512: c * ncols + jc * 512 + w],
                   ("stage", "v_b1"), ("B0",), start=(c == 0), stop=(c == 7),
                   tag=f"bmm_{wname}{jc}{c}")
            dma("qsp", srow[0:1, 0:w], dW[bname][0:1, jc * 512: jc * 512 + w],
                ("srow",), ("srow",), f"ldb_{bname}{jc}")
            tt(srow[0:1, 0:w], B[0][0:1, 0:w], srow[0:1, 0:w], ALU.add,
               ("B0", "srow"), ("srow",), tag=f"badd_{wname}{jc}")
            ts(brow[0:1, brow_off + jc * 512: brow_off + jc * 512 + w], srow[0:1, 0:w],
               1.0 if scale is None else scale, ALU.mult, ("srow",),
               (f"brow_{wname}",), tag=f"bscl_{wname}{jc}")

    prep_weight("Wq", "bq", 1024, Wq_s, lambda c: c * 1024, 0, scale=0.125)
    prep_weight("Wk", "bk", 256, Wkv_s, lambda c: c * 512, 1024)
    prep_weight("Wv", "bv", 256, Wkv_s, lambda c: c * 512 + 256, 1280)

    for c in range(8):
        mm(B[1][0:1, 0:8], C["onescol"][:, :], Wr_s[:, c * 8:(c + 1) * 8],
           ("Wr_s", "c_onescol"), ("B1",), start=(c == 0), stop=(c == 7), tag=f"S{c}")
    cp(srow2[0:1, 0:8], B[1][0:1, 0:8], ("B1",), ("srow2",))
    mm(B[1][:, 0:8], C["onesrowf"][:, :], srow2[0:1, 0:8], ("srow2", "c_onesrowf"),
       ("B1",), start=True, stop=True, tag="Sbc_mm")
    cp(Sbc[:, :], B[1][:, 0:8], ("B1",), ("Sbc",))

    dma("qsp", bf1tmp[:, :], dW["bf1c"], (), ("bf1tmp",), "ldbf1c")
    for g in range(8):
        dma("qsp", stage[:, 0:8 * 512].rearrange("p (c n) -> p c n", c=8),
            dW["Wf1"][:, g * 512:(g + 1) * 512].rearrange("(c p) n -> p c n", p=128),
            ("stage",), ("stage",), f"stgWf1_{g}")
        for c in range(8):
            ts(stg_o[:, c * 512:(c + 1) * 512], stage[:, c * 512:(c + 1) * 512],
               gb["g2"][:, c:c + 1], ALU.mult, ("stage", "v_g2"), ("stg_o",),
               tag=f"sclWf1_{g}{c}")
        for jq in range(4):
            jf = g * 4 + jq
            for c in range(8):
                mm(B[2][:, jf:jf + 1], stage[:, c * 512 + jq * 128: c * 512 + (jq + 1) * 128],
                   gb["b2"][:, c:c + 1], ("stage", "v_b2"), ("B2",),
                   start=(c == 0), stop=(c == 7), tag=f"bWf1_{jf}{c}")
        dma("qact", d_wf1p.rearrange("p (c f) -> p c f", c=8)[:, :, g * 512:(g + 1) * 512],
            stg_o[:, :].rearrange("p (c f) -> p c f", c=8), ("stg_o",),
            (f"wf1p_{g}",), f"stWf1p_{g}")
    tt(bf1col[:, :], B[2][:, 0:32], bf1tmp[:, :], ALU.add, ("B2", "bf1tmp"),
       ("bf1col",), tag="bf1col")

    if debug:
        dma("qsp", dbg["Wkv"], Wkv_s[:, :], ("W_Wk", "W_Wv"), (), "dbgWkv")
        dma("qsp", dbg["brow"], brow[0:1, :], ("brow_Wq", "brow_Wk", "brow_Wv"), (), "dbgbrow")

    def ln_stats(src_keys, src_ap, s4, o_negmu, o_rstd, o_std, tag):
        st = stat[s4]
        bk = f"bn{s4}"
        bn(st[:, 0:6], src_ap[:, 0:512], src_keys + (bk,), (bk,), tag=f"{tag}_bn0")
        bn(st[:, 6:12], src_ap[:, 512:1024], src_keys + (bk,), (bk,), tag=f"{tag}_bn1")
        bnag(st[:, 12:14], st[:, 0:6], (bk,), (bk,), tag=f"{tag}_ag0")
        bnag(st[:, 14:16], st[:, 6:12], (bk,), (bk,), tag=f"{tag}_ag1")
        tt(st[:, 16:17], st[:, 12:13], st[:, 14:15], ALU.add, (bk,), (bk,),
           tag=f"{tag}_ms")
        ts(musd[s4][:, o_negmu:o_negmu + 1], st[:, 16:17], -0.5, ALU.mult,
           (bk,), (f"musd{s4}_{o_negmu}",), tag=f"{tag}_mu")
        tt(st[:, 17:18], st[:, 12:13], st[:, 14:15], ALU.subtract, (bk,), (bk,),
           tag=f"{tag}_md")
        tt(st[:, 18:19], st[:, 17:18], st[:, 17:18], ALU.mult, (bk,), (bk,),
           tag=f"{tag}_d2")
        tt(st[:, 19:20], st[:, 13:14], st[:, 15:16], ALU.add, (bk,), (bk,),
           tag=f"{tag}_vs")
        stt(st[:, 20:21], st[:, 18:19], 0.5, st[:, 19:20], ALU.mult, ALU.add,
            (bk,), (bk,), tag=f"{tag}_2v")
        act(st[:, 21:22], st[:, 20:21], AF.Ln, (bk, "c_epsb"), (bk,),
            bias=C["epsb"][:, 0:1], scale=0.5, tag=f"{tag}_ln")
        act(musd[s4][:, o_rstd:o_rstd + 1], st[:, 21:22], AF.Exp, (bk,),
            (f"musd{s4}_{o_rstd}",), scale=-0.5, tag=f"{tag}_rstd")
        if o_std is not None:
            act(musd[s4][:, o_std:o_std + 1], st[:, 21:22], AF.Exp, (bk,),
                (f"musd{s4}_{o_std}",), scale=0.5, tag=f"{tag}_std")

    def a1_ldx(j):
        dma("qsp", x_sb[j % 2][:, :], dx[j * 128:(j + 1) * 128, :],
            (), (f"x_sb{j % 2}",), f"ldx{j}")

    def a1_pre(j):
        s2, s4 = j % 2, j % 4
        xk = f"x_sb{s2}"
        ln_stats((xk,), x_sb[s2][:, :], s4, 0, 1, None, f"ln0_{j}")
        ts(xn_sb[s4][:, :], x_sb[s2][:, :], musd[s4][:, 0:1], ALU.add,
           (xk, f"musd{s4}_0", f"musd{s4}_1"), (f"xnsb{s4}",),
           s2=musd[s4][:, 1:2], op1=ALU.mult, tag=f"xn{j}")
        dma("qact", d_xn[j * 128:(j + 1) * 128, :], xn_sb[s4][:, :], (f"xnsb{s4}",),
            (f"spxn_{j}",), f"spxn_{j}")
        ln_stats((f"xnsb{s4}",), xn_sb[s4][:, :], s4, 2, 3, 4, f"ln1_{j}")
        ts(base1[s2][:, :], xn_sb[s4][:, :], musd[s4][:, 2:3], ALU.add,
           (f"xnsb{s4}", f"musd{s4}_2", f"musd{s4}_3"), (f"base1_{s2}",),
           s2=musd[s4][:, 3:4], op1=ALU.mult, tag=f"base1_{j}")

    def a1_mid(j):
        s2, s4 = j % 2, j % 4
        for c in range(8):
            transp(pT[:, c * 128:(c + 1) * 128], base1[s2][:, c * 128:(c + 1) * 128],
                   (f"base1_{s2}",), ("pT",), tag=f"tb_{j}{c}")
        for c in range(8):
            cp(baseT[s2][:, c * 128:(c + 1) * 128], pT[:, c * 128:(c + 1) * 128],
               ("pT",), (f"baseT{s2}_{c}",), tag=f"tbe_{j}{c}")
        for c in range(8):
            lk = (f"baseT{s2}_{c}", "W_Wq", "W_Wk", "W_Wv", "Wr_s")
            lh = baseT[s2][:, c * 128:(c + 1) * 128]
            mm(B[0][:, :], lh, Wq_s[:, c * 1024:c * 1024 + 512], lk, ("B0",),
               start=(c == 0), stop=False, tag=f"q0_{j}{c}")
            mm(B[1][:, :], lh, Wq_s[:, c * 1024 + 512:(c + 1) * 1024], lk, ("B1",),
               start=(c == 0), stop=False, tag=f"q1_{j}{c}")
            mm(B[2][:, 0:256], lh, Wkv_s[:, c * 512:c * 512 + 256], lk, ("B2",),
               start=(c == 0), stop=False, tag=f"k_{j}{c}")
            mm(B[2][:, 256:512], lh, Wkv_s[:, c * 512 + 256:(c + 1) * 512], lk, ("B2",),
               start=False, stop=False, tag=f"v_{j}{c}")
            mm(B[3][:, 0:8], lh, Wr_s[:, c * 8:(c + 1) * 8], lk, ("B3",),
               start=(c == 0), stop=(c == 7), tag=f"r_{j}{c}")
        brd = ("brow_Wq", "brow_Wk", "brow_Wv", "c_onesrow")
        mm(B[0][:, :], C["onesrow"][:, :], brow[0:1, 0:512], brd, ("B0",),
           start=False, stop=True, tag=f"qb0_{j}")
        mm(B[1][:, :], C["onesrow"][:, :], brow[0:1, 512:1024], brd, ("B1",),
           start=False, stop=True, tag=f"qb1_{j}")
        mm(B[2][:, 0:256], C["onesrow"][:, :], brow[0:1, 1024:1280], brd, ("B2",),
           start=False, stop=True, tag=f"kb_{j}")
        mm(B[2][:, 256:512], C["onesrow"][:, :], brow[0:1, 1280:1536], brd, ("B2",),
           start=False, stop=True, tag=f"vb_{j}")
        ts(muS[s4][:, :], Sbc[:, :], musd[s4][:, 2:3], ALU.mult,
           ("Sbc", f"musd{s4}_2"), (f"muS{s4}",), tag=f"muS{j}")
        ts(muS[s4][:, :], muS[s4][:, :], -1.0, ALU.mult, (f"muS{s4}",), (f"muS{s4}",))
        stt(logits_all[:, j * 8:(j + 1) * 8], B[3][:, 0:8], musd[s4][:, 4:5],
            muS[s4][:, :], ALU.mult, ALU.add,
            ("B3", f"musd{s4}_4", f"muS{s4}"), (f"logit{j}", "logits_all"), tag=f"logf{j}")
        act(qtmp[s2][:, 0:512], B[0][:, :], AF.Copy, ("B0",), (f"qe0_{s2}",), tag=f"qe0_{j}")
        act(qtmp[s2][:, 512:1024], B[1][:, :], AF.Copy, ("B1",), (f"qe1_{s2}",), tag=f"qe1_{j}")
        act(qtmp[s2][:, 1024:1280], B[2][:, 0:256], AF.Copy, ("B2",), (f"ke_{s2}",), tag=f"ke_{j}")
        act(bass.AP(vsb[j % 4], 0, [vsb[j % 4][:, :].ap[0], [65, 4], [1, 64]]),
            B[2][:, 256:512].rearrange("p (g d) -> p g d", g=4), AF.Copy,
            ("B2", f"v{j % 4}"), (f"v{j % 4}",), tag=f"ve_{j}")
        cosb = C["cost"][:, j * 32:(j + 1) * 32].unsqueeze(1).broadcast_to([128, 16, 32])
        sinb = C["sint"][:, j * 32:(j + 1) * 32].unsqueeze(1).broadcast_to([128, 16, 32])
        cosb4 = C["cost"][:, j * 32:(j + 1) * 32].unsqueeze(1).broadcast_to([128, 4, 32])
        sinb4 = C["sint"][:, j * 32:(j + 1) * 32].unsqueeze(1).broadcast_to([128, 4, 32])
        qv = qtmp[s2][:, 0:1024].rearrange("p (h d) -> p h d", h=16)
        q1, q2 = qv[:, :, 0:32], qv[:, :, 32:64]
        ro = ropeq[s2][:, :].rearrange("p (h d) -> p h d", h=16)
        ta = tmpA[s2][:, :].rearrange("p (h d) -> p h d", h=16)
        tb = tmpB[s2][:, :].rearrange("p (h d) -> p h d", h=16)
        qks = (f"qe0_{s2}", f"qe1_{s2}")
        tt(ta, q1, cosb, ALU.mult, qks + ("c_cost",), (f"tmpA{s2}",), tag=f"r1_{j}")
        tt(tb, q2, sinb, ALU.mult, qks + ("c_sint",), (f"tmpB{s2}",), tag=f"r2_{j}")
        tt(ro[:, :, 0:32], ta, tb, ALU.subtract, (f"tmpA{s2}", f"tmpB{s2}"),
           (f"ropeq{s2}",), tag=f"r3_{j}")
        tt(ta, q2, cosb, ALU.mult, qks + ("c_cost",), (f"tmpA{s2}",), tag=f"r4_{j}")
        tt(tb, q1, sinb, ALU.mult, qks + ("c_sint",), (f"tmpB{s2}",), tag=f"r5_{j}")
        tt(ro[:, :, 32:64], ta, tb, ALU.add, (f"tmpA{s2}", f"tmpB{s2}"),
           (f"ropeq{s2}",), tag=f"r6_{j}")
        kv = qtmp[s2][:, 1024:1280].rearrange("p (h d) -> p h d", h=4)
        k1, k2 = kv[:, :, 0:32], kv[:, :, 32:64]
        rk = ropek[s2][:, :].rearrange("p (h d) -> p h d", h=4)
        ka = krA[s2][:, :].rearrange("p (h d) -> p h d", h=4)
        kb = krB[s2][:, :].rearrange("p (h d) -> p h d", h=4)
        kk_ = f"ke_{s2}"
        tt(ka, k1, cosb4, ALU.mult, (kk_, "c_cost"), (f"krA{s2}",), tag=f"k1_{j}")
        tt(kb, k2, sinb4, ALU.mult, (kk_, "c_sint"), (f"krB{s2}",), tag=f"k2_{j}")
        tt(rk[:, :, 0:32], ka, kb, ALU.subtract, (f"krA{s2}", f"krB{s2}"),
           (f"ropek{s2}",), tag=f"k3_{j}")
        tt(ka, k2, cosb4, ALU.mult, (kk_, "c_cost"), (f"krA{s2}",), tag=f"k4_{j}")
        tt(kb, k1, sinb4, ALU.mult, (kk_, "c_sint"), (f"krB{s2}",), tag=f"k5_{j}")
        tt(rk[:, :, 32:64], ka, kb, ALU.add, (f"krA{s2}", f"krB{s2}"),
           (f"ropek{s2}",), tag=f"k6_{j}")

    def a1_back(j):
        s2, s4 = j % 2, j % 4
        for r in range(8):
            hlo, hhi = Q_LO[r], Q_HI[r]
            transp(pT[0:64, r * 128:(r + 1) * 128], ropeq[s2][:, hlo * 64:hlo * 64 + 64],
                   (f"ropeq{s2}",), ("pT",), tp=(0, 0), tag=f"tq_{j}{r}l")
            transp(pT[64:128, r * 128:(r + 1) * 128], ropeq[s2][:, hhi * 64:hhi * 64 + 64],
                   (f"ropeq{s2}",), ("pT",), tp=(0, 64), tag=f"tq_{j}{r}h")
        for r in range(8):
            cp(qT[s2][:, r * 128:(r + 1) * 128], pT[:, r * 128:(r + 1) * 128],
               ("pT",), (f"qT{s2}_{r}",), tag=f"tqe_{j}{r}")
        for cch in range(2):
            transp(pT[:, cch * 128:(cch + 1) * 128],
                   ropek[s2][:, cch * 128:(cch + 1) * 128],
                   (f"ropek{s2}",), ("pT",), tag=f"tk_{j}{cch}")
        for cch in range(2):
            cp(kT[j % 4][:, cch * 128:(cch + 1) * 128], pT[:, cch * 128:(cch + 1) * 128],
               ("pT",), (f"kT{j % 4}",), tag=f"tke_{j}{cch}")
        kks = [kx for kx in (j - 2, j - 1, j) if kx >= 0]
        for g in range(4):
            asl = g % 2
            b0 = 64 * (g % 2)
            ccol = 128 * (g // 2)
            for ki, kk in enumerate(kks):
                dmask = j - kk
                lhsT = kT[kk % 4][b0:b0 + 64, ccol:ccol + 128]
                for h4 in range(4):
                    h = 4 * g + h4
                    rhs = qT[s2][b0:b0 + 64, Q_REG[h] * 128:(Q_REG[h] + 1) * 128]
                    mm(B[4 + ki][:, h4 * 128:(h4 + 1) * 128], lhsT, rhs,
                       (f"kT{kk % 4}", f"qT{s2}_{Q_REG[h]}"), (f"B{4 + ki}",),
                       start=(h4 == 0), stop=(dmask == 1 and h4 == 3),
                       tag=f"sc_{j}g{g}k{ki}h{h4}")
                if dmask != 1:
                    mrhs = C["m0"][:, :] if dmask == 0 else C["m2"][:, :]
                    mk = "c_m0" if dmask == 0 else "c_m2"
                    for h4 in range(4):
                        mm(B[4 + ki][:, h4 * 128:(h4 + 1) * 128], C["ident"][:, :], mrhs,
                           ("c_ident", mk), (f"B{4 + ki}",), start=False, stop=(h4 == 3),
                           tag=f"mk_{j}g{g}k{ki}h{h4}")
            for ki in range(len(kks)):
                act(attnT[asl][:, ki * 512:(ki + 1) * 512], B[4 + ki][:, :], AF.Exp,
                    (f"B{4 + ki}",), (f"attnT{asl}",), tag=f"exp_{j}g{g}k{ki}")
            for h4 in range(4):
                for ki, kk in enumerate(kks):
                    mm(B[3][:, 64 + h4 * 65: 64 + (h4 + 1) * 65],
                       attnT[asl][:, ki * 512 + h4 * 128: ki * 512 + (h4 + 1) * 128],
                       vsb[kk % 4][:, g * 65:(g + 1) * 65],
                       (f"attnT{asl}", f"v{kk % 4}"), ("B3",),
                       start=(h4 == 0 and ki == 0),
                       stop=(h4 == 3 and ki == len(kks) - 1), tag=f"pv_{j}g{g}h{h4}k{ki}")
            P.op("dve", lambda e, o=rrec[asl][:, 0:4],
                 i=bass.AP(B[3], 128, [B[3][:, :].ap[0], [65, 4], [1, 1]]):
                 e.reciprocal(o, i), ("B3",), (f"rrec{asl}",), f"rc_{j}g{g}")
            for h4 in range(4):
                h = 4 * g + h4
                act(attn_out[s2][:, h * 64:(h + 1) * 64],
                    B[3][:, 64 + h4 * 65: 64 + h4 * 65 + 64], AF.Copy,
                    ("B3", f"rrec{asl}"), (f"attn_out{s2}",),
                    scale=rrec[asl][:, h4:h4 + 1], tag=f"oev_{j}g{g}h{h4}")
        for c in range(8):
            transp(pT[:, c * 128:(c + 1) * 128], attn_out[s2][:, c * 128:(c + 1) * 128],
                   (f"attn_out{s2}",), ("pT",), tag=f"tao_{j}{c}")
        for c in range(8):
            cp(aoT[s2][:, c * 128:(c + 1) * 128], pT[:, c * 128:(c + 1) * 128],
               ("pT",), (f"aoT{s2}_{c}",), tag=f"taoe_{j}{c}")
        for c in range(8):
            lh = aoT[s2][:, c * 128:(c + 1) * 128]
            mm(B[0][:, :], lh, Wo_s[:, c * 1024:c * 1024 + 512],
               (f"aoT{s2}_{c}", "Wo_s"), ("B0",), start=(c == 0), stop=False, tag=f"wo0_{j}{c}")
            mm(B[1][:, :], lh, Wo_s[:, c * 1024 + 512:(c + 1) * 1024],
               (f"aoT{s2}_{c}", "Wo_s"), ("B1",), start=(c == 0), stop=False, tag=f"wo1_{j}{c}")
        mm(B[0][:, :], C["onesrow"][:, :], borow[0:1, 0:512], ("borow", "c_onesrow"),
           ("B0",), start=False, stop=True, tag=f"wob0_{j}")
        mm(B[1][:, :], C["onesrow"][:, :], borow[0:1, 512:1024], ("borow", "c_onesrow"),
           ("B1",), start=False, stop=True, tag=f"wob1_{j}")
        tt(x2sb[s2][:, 0:512], B[0][:, :], xn_sb[s4][:, 0:512], ALU.add,
           ("B0", f"xnsb{s4}", f"x2_{s2}a"), (f"x2_{s2}a",), tag=f"x2a_{j}")
        tt(x2sb[s2][:, 512:1024], B[1][:, :], xn_sb[s4][:, 512:1024], ALU.add,
           ("B1", f"xnsb{s4}", f"x2_{s2}b"), (f"x2_{s2}b",), tag=f"x2b_{j}")
        x2k = (f"x2_{s2}a", f"x2_{s2}b")
        ln_stats(x2k, x2sb[s2][:, :], s4, 5, 6, None, f"ln2_{j}")
        ts(base2[s2][:, :], x2sb[s2][:, :], musd[s4][:, 5:6], ALU.add,
           x2k + (f"musd{s4}_5", f"musd{s4}_6"), (f"base2_{s2}",),
           s2=musd[s4][:, 6:7], op1=ALU.mult, tag=f"b2_{j}")
        dma("qact", d_x2[j * 128:(j + 1) * 128, :], x2sb[s2][:, :], x2k,
            (f"spx2_{j}",), f"spx2_{j}")
        dma("qact", d_b2s[j * 128:(j + 1) * 128, :], base2[s2][:, :], (f"base2_{s2}",),
            (f"spb2_{j}",), f"spb2_{j}")

    P.alias("sq0", "stg_o")
    P.alias("sq1", "stg_o")
    a1_ldx(0)
    a1_ldx(1)
    a1_pre(0)
    a1_ldx(2)
    a1_pre(1)
    a1_mid(0)
    a1_ldx(3)
    a1_pre(2)
    a1_mid(1)
    for j in range(NT):
        a1_back(j)
        if j + 4 < NT:
            a1_ldx(j + 4)
        if j + 3 < NT:
            a1_pre(j + 3)
        if j + 2 < NT:
            a1_mid(j + 2)

    if debug:
        dma("qsp", dbg["xn"], d_xn, tuple(f"spxn_{j}" for j in range(NT)), (), "dbgxn")
        dma("qsp", dbg["logits"], logits_all[:, :],
            tuple(f"logit{j}" for j in range(NT)), (), "dbglog")
        dma("qsp", dbg["x2"], d_x2, tuple(f"spx2_{j}" for j in range(NT)), (), "dbgx2")

    LA = ("logits_all",)
    act(rt_f[:, :], logits_all[:, :], AF.Exp, LA, ("rt_f",), tag="rexp")
    red(zrec[:, :], rt_f[:, :].rearrange("p (a b) -> p a b", a=NT), ALU.add,
        ("rt_f",), ("zsum",), tag="rz")
    P.op("dve", lambda e: e.reciprocal(zrec[:, :], zrec[:, :]), ("zsum",), ("zrec",), "rzr")
    red(max1[:, :], logits_all[:, :].rearrange("p (a b) -> p a b", a=NT), ALU.max,
        LA, ("max1",), tag="rm1")
    tt(m1_all[:, :].rearrange("p (a b) -> p a b", a=NT),
       logits_all[:, :].rearrange("p (a b) -> p a b", a=NT),
       max1[:, :].unsqueeze(2).broadcast_to([128, NT, 8]), ALU.is_ge,
       LA + ("max1",), ("m1_all",), tag="rm1h")
    stt(rt_g[:, :], m1_all[:, :], NEG, logits_all[:, :], ALU.mult, ALU.add,
        ("m1_all",) + LA, ("rt_g",), tag="rmask")
    red(max2[:, :], rt_g[:, :].rearrange("p (a b) -> p a b", a=NT), ALU.max,
        ("rt_g",), ("max2",), tag="rm2")
    tt(m2_all[:, :].rearrange("p (a b) -> p a b", a=NT),
       rt_g[:, :].rearrange("p (a b) -> p a b", a=NT),
       max2[:, :].unsqueeze(2).broadcast_to([128, NT, 8]), ALU.is_ge,
       ("rt_g", "max2"), ("m2_all",), tag="rm2h")
    act(p1sb[:, :], max1[:, :], AF.Exp, ("max1",), ("p1e",), tag="rp1e")
    tt(p1sb[:, :], p1sb[:, :], zrec[:, :], ALU.mult, ("p1e", "zrec"), ("p1sb",), tag="rp1")
    act(p2sb[:, :], max2[:, :], AF.Exp, ("max2",), ("p2e",), tag="rp2e")
    tt(p2sb[:, :], p2sb[:, :], zrec[:, :], ALU.mult, ("p2e", "zrec"), ("p2sb",), tag="rp2")
    for snum, mall, mkey in ((0, m1_all, "m1_all"), (1, m2_all, "m2_all")):
        cb = B[4 + snum]
        for j in range(NT):
            mm(cb[:, j * 8:(j + 1) * 8], C["u128"][:, :], mall[:, j * 8:(j + 1) * 8],
               ("c_u128", mkey), (f"B{4 + snum}",), start=True, stop=True,
               tag=f"cs_{snum}_{j}")
            mm(B[6][0:1, snum * 256 + j * 8: snum * 256 + (j + 1) * 8],
               C["onescol"][:, :], mall[:, j * 8:(j + 1) * 8],
               ("c_onescol", mkey), ("B6",), start=True, stop=True,
               tag=f"ct_{snum}_{j}")
    crowA, crowB = ssb[1][0:1, 0:1024], ssb[2][0:1, 0:1024]
    memset(crowA, 0.0, ("carrow",), ("carrow",), "zcarA")
    memset(crowB, 0.0, ("carrow2",), ("carrow2",), "zcarB")
    cp(crowA[0:1, 512:1024], B[6][0:1, 0:512], ("B6",), ("carrow",), tag="ct_ev")
    cur, curk, oth, othk = crowA, "carrow", crowB, "carrow2"
    for st_ in range(6):
        d_ = 8 * (1 << st_)
        tt(oth[0:1, 512:1024], cur[0:1, 512:1024], cur[0:1, 512 - d_:1024 - d_],
           ALU.add, (curk,), (othk,), tag=f"pfx{st_}")
        cur, curk, oth, othk = oth, othk, cur, curk
    mm(B[3][:, 0:256], C["onesrowf"][:, :], cur[0:1, 504:760],
       (curk, "c_onesrowf"), ("B3",), start=True, stop=True, tag="cbc1")
    cp(rt_f[:, 0:256], B[3][:, 0:256], ("B3",), ("rt_f",), tag="cbc1e")
    mm(B[3][:, 0:256], C["onesrowf"][:, :], cur[0:1, 760:1016],
       (curk, "c_onesrowf"), ("B3",), start=True, stop=True, tag="cbc2")
    cp(rt_g[:, 0:256], B[3][:, 0:256], ("B3",), ("rt_g",), tag="cbc2e")
    for snum, mkey in ((0, "m1_all"), (1, "m2_all")):
        mall = m1_all if snum == 0 else m2_all
        crep = rt_f if snum == 0 else rt_g
        crepk = "rt_f" if snum == 0 else "rt_g"
        stt(rt_h[:, :], B[4 + snum][:, 0:NT * 8], 0.0, crep[:, 0:256],
            ALU.add, ALU.add, (f"B{4 + snum}", crepk), ("rt_h",), tag=f"pc_{snum}")
        tt(rt_h[:, :], rt_h[:, :], mall[:, :], ALU.mult,
           ("rt_h", mkey), ("rt_h",), tag=f"pm_{snum}")
        red(poscol[:, :], rt_h[:, :].rearrange("p (a b) -> p a b", a=NT), ALU.add,
            ("rt_h",), ("poscol",), tag=f"pr_{snum}")
        tt(rt_h[:, :].rearrange("p (a b) -> p a b", a=NT),
           mall[:, :].rearrange("p (a b) -> p a b", a=NT),
           C["iota8"][:, :].unsqueeze(1).broadcast_to([128, NT, 8]), ALU.mult,
           (mkey, "c_iota8"), ("rt_h",), tag=f"em_{snum}")
        red(ecol[:, :], rt_h[:, :].rearrange("p (a b) -> p a b", a=NT), ALU.add,
            ("rt_h",), ("ecol",), tag=f"er_{snum}")
        stt(idxf[:, :], ecol[:, :], float(CAP), poscol[:, :], ALU.mult, ALU.add,
            ("ecol", "poscol"), ("idxf",), tag=f"ix_{snum}")
        ts(idxf[:, :], idxf[:, :], -1.0, ALU.add, ("idxf",), ("idxf",), tag=f"ix1_{snum}")
        ts(bumpc[:, :], poscol[:, :], float(CAP) + 0.5, ALU.is_ge, ("poscol",),
           ("bumpc",), s2=1e6, op1=ALU.mult, tag=f"bp_{snum}")
        tt(idxf[:, :], idxf[:, :], bumpc[:, :], ALU.add, ("idxf", "bumpc"), ("idxf",),
           tag=f"ix2_{snum}")
        cp(idxu[snum][:, :], idxf[:, :], ("idxf",), (f"idxu{snum}",), tag=f"iu_{snum}")
    if debug:
        dma("qsp", dbg["idx1"], idxu[0][:, :], ("idxu0",), (), "dbgi1")
        dma("qsp", dbg["idx2"], idxu[1][:, :], ("idxu1",), (), "dbgi2")
        dma("qsp", dbg["p1"], p1sb[:, :], ("p1sb",), (), "dbgp1")
        dma("qsp", dbg["p2"], p2sb[:, :], ("p2sb",), (), "dbgp2")

    for j in range(NT):
        for snum in range(2):
            P.op("qindS", lambda e, i=C["tokids"][:, j:j + 1], ix=idxu[snum][:, j:j + 1]:
                 e.indirect_dma_start(out=d_tok, out_offset=bass.IndirectOffsetOnAxis(
                     ap=ix, axis=0), in_=i, in_offset=None,
                     bounds_check=breg, oob_is_err=False),
                 ("c_tokids", f"idxu{snum}", "d_tok_fill", "breg"), (f"tsc_{j}_{snum}",),
                 tag=f"scat{j}_{snum}")
    scat_done = tuple(f"tsc_{j}_{s}" for j in range(NT) for s in range(2))
    if debug:
        dma("qsp", dbg["tok"], d_tok, scat_done, (), "dbgtok")

    for c in range(8):
        P.alias(f"b2T_{c}", "stg_o", "sq0", "sq1")
    for jf in range(32):
        P.alias(f"gT{jf}", "stage", "stg_o")
    P.alias("ssb0_0", "srow", "srow2")
    P.alias("ssb0_1", "srow", "srow2")
    P.alias("ssb1_0", "carrow")
    P.alias("ssb1_1", "carrow")
    P.alias("ssb2_0", "carrow2")
    P.alias("ssb2_1", "carrow2")
    P.alias("wf2b0", "Wo_s")
    P.alias("wf2b1", "Wo_s")
    for i in range(4):
        P.alias(f"b2blk{i}", "W_Wk", "W_Wv", "brow_Wk", "brow_Wv")
    for n in range(8):
        for i in range(4):
            jj = 4 * n + i
            dma("qsp", b2blk[i][:, :], d_b2s[jj * 128:(jj + 1) * 128, :],
                (f"spb2_{jj}",), (f"b2blk{i}",), f"ldb2_{jj}")
            dma("qsp", x2blk[i][:, :], d_x2[jj * 128:(jj + 1) * 128, :],
                (f"spx2_{jj}",),
                (f"x2blk{i}",) + ((f"x2_{i}a", f"x2_{i}b")
                                  if i < 2 else (f"x_sb{i - 2}",)), f"ldx2_{jj}")
        for i in range(4):
            for c in range(8):
                transp(pT[:, c * 128:(c + 1) * 128], b2blk[i][:, c * 128:(c + 1) * 128],
                       (f"b2blk{i}",), ("pT",), tag=f"tb2_{n}{i}{c}")
            for c in range(8):
                cp(b2T[:, c * 512 + i * 128: c * 512 + (i + 1) * 128],
                   pT[:, c * 128:(c + 1) * 128], ("pT",), (f"b2T_{c}",),
                   tag=f"tb2e_{n}{i}{c}")
        for gi in range(8):
            dma("qsp", wf1buf[gi % 2][:, :],
                d_wf1p.rearrange("p (c f) -> p c f", c=8)[:, :, gi * 512:(gi + 1) * 512],
                (f"wf1b{gi % 2}", f"wf1p_{gi}"), (f"wf1b{gi % 2}",), f"ldwf1_{n}{gi}")
            for jq in range(4):
                jf = gi * 4 + jq
                bk = B[jf % 3]
                bkk = f"B{jf % 3}"
                for c in range(8):
                    mm(bk[:, :], wf1buf[gi % 2][:, c * 512 + jq * 128: c * 512 + (jq + 1) * 128],
                       b2T[:, c * 512:(c + 1) * 512],
                       (f"wf1b{gi % 2}", f"b2T_{c}"), (bkk,),
                       start=(c == 0), stop=(c == 7), tag=f"f1_{n}{jf}{c}")
                rd = (bkk, "bf1col")
                act(gT[:, jf * 512:(jf + 1) * 512], bk[:, :], AF.Gelu, rd, (f"gT{jf}",),
                    bias=bf1col[:, jf:jf + 1], tag=f"gel_{n}{jf}")
        for half in range(2):
            for jg in range(8):
                dma(f"qp_wf2b{jg % 2}", wf2buf[jg % 2][:, :].rearrange("p (c f) -> p c f", c=4),
                    dW["Wf2"][jg * 512:(jg + 1) * 512, half * 512:(half + 1) * 512]
                    .rearrange("(c p) f -> p c f", p=128),
                    (f"wf2b{jg % 2}",), (f"wf2b{jg % 2}",), f"ldwf2_{n}{half}{jg}")
                for jq in range(4):
                    jf = jg * 4 + jq
                    for tsl in range(4):
                        mm(B[3 + tsl][:, :],
                           gT[:, jf * 512 + tsl * 128: jf * 512 + (tsl + 1) * 128],
                           wf2buf[jg % 2][:, jq * 512:(jq + 1) * 512],
                           (f"gT{jf}", f"wf2b{jg % 2}"), (f"B{3 + tsl}",),
                           start=(jf == 0), stop=False, tag=f"f2_{n}{half}{jf}{tsl}")
            for tsl in range(4):
                mm(B[3 + tsl][:, :], C["onesrow"][:, :],
                   bf2row[0:1, half * 512:(half + 1) * 512], ("bf2row", "c_onesrow"),
                   (f"B{3 + tsl}",), start=False, stop=True, tag=f"f2b_{n}{half}{tsl}")
            for tsl in range(4):
                jj = 4 * n + tsl
                tt(ssb[tsl][:, half * 512:(half + 1) * 512], B[3 + tsl][:, :],
                   x2blk[tsl][:, half * 512:(half + 1) * 512], ALU.add,
                   (f"B{3 + tsl}", f"x2blk{tsl}"), (f"ssb{tsl}_{half}",),
                   tag=f"sev_{n}{half}{tsl}")
                if half == 1:
                    dma("qact", d_s[jj * 128:(jj + 1) * 128, :], ssb[tsl][:, :],
                        (f"ssb{tsl}_0", f"ssb{tsl}_1"), (f"sps_{jj}",), f"sps_{jj}")
    if debug:
        dma("qsp", dbg["s"], d_s, tuple(f"sps_{jj}" for jj in range(NT)), (), "dbgs")

    P.alias("eib0", "W_Wk", "W_Wv", "b2blk0", "b2blk1")
    P.alias("eib1", "W_Wk", "W_Wv", "b2blk1", "b2blk2", "b2blk3")
    for c in range(8):
        P.alias(f"eiT_{c}", "W_Wq", "brow_Wq")
    P.alias("wib1", "W_Wq")
    P.alias("wib0", *[f"b2T_{c}" for c in range(8)])
    P.alias("eob0", "base1_0")
    P.alias("eob1", "base1_1")
    P.alias("eob2", "baseT0")
    P.alias("eob3", "baseT1")
    P.alias("wob0", "Wo_s", "wf2b0", "wf2b1")
    P.alias("wob1", "Wo_s", "wf2b0", "wf2b1")
    xn_done = tuple(f"spxn_{j}" for j in range(NT))
    for e_ in range(NE):
        for i in range(4):
            ii = i % 2
            ti = 4 * e_ + i
            dma("qsp", tokcol[ii][:, :], d_tok[ti * 128:(ti + 1) * 128, :],
                (f"tokcol{ii}",) + (scat_done if e_ == 0 and i < 2 else ()),
                (f"tokcol{ii}",), f"ldtok_{e_}{i}")
            memset(eib[ii][:, :], 0.0, (f"eib{ii}",), (f"eib{ii}",), f"zei_{e_}{i}")
            P.op(f"qindE{ii}", lambda e, o=eib[ii][:, :], ix=tokcol[ii][:, :]:
                 e.indirect_dma_start(out=o, out_offset=None, in_=d_xn,
                     in_offset=bass.IndirectOffsetOnAxis(ap=ix, axis=0),
                     bounds_check=breg, oob_is_err=False),
                 (f"tokcol{ii}", f"eib{ii}", "breg") + (xn_done if e_ == 0 and i < 2 else ()),
                 (f"eib{ii}",), f"gei_{e_}{i}")
            for c in range(8):
                transp(pT[:, c * 128:(c + 1) * 128], eib[ii][:, c * 128:(c + 1) * 128],
                       (f"eib{ii}",), ("pT",), tag=f"tei_{e_}{i}{c}")
            for c in range(8):
                cp(eiT[:, c * 512 + i * 128: c * 512 + (i + 1) * 128],
                   pT[:, c * 128:(c + 1) * 128], ("pT",), (f"eiT_{c}",),
                   tag=f"teie_{e_}{i}{c}")
        for gi in range(8):
            dma(f"qp_wib{gi % 2}", wib[gi % 2][:, :].rearrange("p (c f) -> p c f", c=8),
                dW["wi2"][e_ * D:(e_ + 1) * D, gi * 512:(gi + 1) * 512]
                .rearrange("(c p) f -> p c f", p=128),
                (f"wib{gi % 2}",), (f"wib{gi % 2}",), f"ldwi_{e_}{gi}")
            for jq in range(4):
                jf = gi * 4 + jq
                bk = B[jf % 3]
                bkk = f"B{jf % 3}"
                for c in range(8):
                    mm(bk[:, :], wib[gi % 2][:, c * 512 + jq * 128: c * 512 + (jq + 1) * 128],
                       eiT[:, c * 512:(c + 1) * 512],
                       (f"wib{gi % 2}", f"eiT_{c}"), (bkk,),
                       start=(c == 0), stop=(c == 7), tag=f"wi_{e_}{jf}{c}")
                act(gT[:, jf * 512:(jf + 1) * 512], bk[:, :], AF.Gelu_apprx_tanh,
                    (bkk,), (f"gT{jf}",), tag=f"gt_{e_}{jf}")
        for half in range(2):
            for jg in range(8):
                dma(f"qp_wob{jg % 2}", wob[jg % 2][:, :].rearrange("p (c f) -> p c f", c=4),
                    dW["wo2"][(e_ * DFF + jg * 512):(e_ * DFF + (jg + 1) * 512),
                              half * 512:(half + 1) * 512]
                    .rearrange("(c p) f -> p c f", p=128),
                    (f"wob{jg % 2}",), (f"wob{jg % 2}",), f"ldwo_{e_}{half}{jg}")
                for jq in range(4):
                    jf = jg * 4 + jq
                    for cs in range(4):
                        mm(B[3 + cs][:, :],
                           gT[:, jf * 512 + cs * 128: jf * 512 + (cs + 1) * 128],
                           wob[jg % 2][:, jq * 512:(jq + 1) * 512],
                           (f"gT{jf}", f"wob{jg % 2}"), (f"B{3 + cs}",),
                           start=(jf == 0), stop=(jf == 31), tag=f"woM_{e_}{half}{jf}{cs}")
            for cs in range(4):
                act(eob4[cs][:, half * 512:(half + 1) * 512],
                    B[3 + cs][:, :], AF.Copy, (f"B{3 + cs}",), (f"eob{cs}",),
                    tag=f"eoe_{e_}{half}{cs}")
        for cs in range(4):
            dma("qact", d_eo[(4 * e_ + cs) * 128:(4 * e_ + cs + 1) * 128, :],
                eob4[cs][:, :], (f"eob{cs}",),
                (f"eost_{e_}_{cs}",), f"steo_{e_}{cs}")
    eo_done = tuple(f"eost_{e_}_{cs}" for e_ in range(NE) for cs in range(4))
    if debug:
        dma("qsp", dbg["eo"], d_eo, eo_done, (), "dbgeo")

    for k in range(4):
        P.alias(f"gth0_{k}", "wf2b0", "wf2b1", "Wo_s")
        P.alias(f"gth1_{k}", "wob0", "wob1", "Wo_s")
    P.alias("moes0", "ssb0_0", "ssb0_1")
    P.alias("moes1", "ssb1_0", "ssb1_1", "carrow")
    P.alias("gsil0", "ssb2_0", "ssb2_1", "carrow2")
    P.alias("gsil1", "ssb3_0", "ssb3_1")
    P.alias("sfin0", "x2blk0", "x2_0a", "x2_0b")
    P.alias("sfin1", "x2blk1", "x2_1a", "x2_1b")
    P.alias("x_sb0", "x2blk2")
    P.alias("x_sb1", "x2blk3")
    for jg in range(NT // 2):
        h = jg % 2
        qch = "qindA" if h == 0 else "qindB"
        gkeys = tuple(f"gth{h}_{k}" for k in range(4))
        for i, j in enumerate((2 * jg, 2 * jg + 1)):
            sl = j % 2
            dma("qsp", x_sb[sl][:, :], dx[j * 128:(j + 1) * 128, :],
                (), (f"x_sb{sl}",), f"ldxf_{j}")
            dma("qsp", sfin[sl][:, :], d_s[j * 128:(j + 1) * 128, :],
                (f"sps_{j}",), (f"sfin{sl}",), f"ldsf_{j}")
            for snum in range(2):
                gk = f"gth{h}_{2 * i + snum}"
                memset(gset[h][2 * i + snum][:, :], 0.0, (gk,), (gk,),
                       f"zg_{j}{snum}")
                P.op(qch, lambda e, o=gset[h][2 * i + snum][:, :],
                     ix=idxu[snum][:, j:j + 1]:
                     e.indirect_dma_start(out=o, out_offset=None, in_=d_eo,
                         in_offset=bass.IndirectOffsetOnAxis(ap=ix, axis=0),
                         bounds_check=breg, oob_is_err=False),
                     (f"idxu{snum}", gk, "breg") + eo_done, (gk,),
                     f"gath{j}_{snum}")
        for i, j in enumerate((2 * jg, 2 * jg + 1)):
            sl = j % 2
            ts(moes[sl][:, :], gset[h][2 * i][:, :], p1sb[:, j:j + 1], ALU.mult,
               gkeys + ("p1sb",), (f"moes{sl}",), tag=f"cmb1_{j}")
            stt(moes[sl][:, :], gset[h][2 * i + 1][:, :], p2sb[:, j:j + 1],
                moes[sl][:, :], ALU.mult, ALU.add,
                (f"gth{h}_{2 * i + 1}", "p2sb", f"moes{sl}"), (f"moes{sl}",),
                tag=f"cmb2_{j}")
            act(gsil[sl][:, :], moes[sl][:, :], AF.Silu, (f"moes{sl}",),
                (f"gsil{sl}",), tag=f"sil_{j}")
            tt(sfin[sl][:, :], sfin[sl][:, :], gsil[sl][:, :], ALU.mult,
               (f"sfin{sl}", f"gsil{sl}"), (f"sfin{sl}",), tag=f"og_{j}")
            tt(sfin[sl][:, :], sfin[sl][:, :], x_sb[sl][:, :], ALU.add,
               (f"sfin{sl}", f"x_sb{sl}"), (f"sfin{sl}",), tag=f"ox_{j}")
            dma("qact", dout[j * 128:(j + 1) * 128, :], sfin[sl][:, :],
                (f"sfin{sl}",), (f"outst_{j}",), f"stout_{j}")

    P.finalize()
    chans = sorted({o.ch for eng in P.stream for o in P.stream[eng]} | set(ENGCH))
    sems = {chn: ctx.enter_context(nc.semaphore("sem_" + chn)) for chn in chans}
    with nc.Block() as block:
        @block.tensor
        def _(e):
            P.emit_engine("pe", e, sems)

        @block.scalar
        def _(e):
            P.emit_engine("act", e, sems)

        @block.vector
        def _(e):
            P.emit_engine("dve", e, sems)

        @block.gpsimd
        def _(e):
            P.emit_engine("pool", e, sems)

        @block.sync
        def _(e):
            P.emit_engine("sp", e, sems)
            for chn, v in sorted(P.final_counts.items()):
                e.wait_ge(sems[chn], v)
    ctx.close()
    return nc


def make_in_maps(inputs):
    x = np.asarray(inputs["L0_attn_x"], np.float32)
    consts = host_consts()
    base = {
        "Wq": np.asarray(inputs["L0_attn_Wq"], np.float32),
        "bq": np.asarray(inputs["L0_attn_bq"], np.float32).reshape(1, -1),
        "Wk": np.asarray(inputs["L0_attn_Wk"], np.float32),
        "bk": np.asarray(inputs["L0_attn_bk"], np.float32).reshape(1, -1),
        "Wv": np.asarray(inputs["L0_attn_Wv"], np.float32),
        "bv": np.asarray(inputs["L0_attn_bv"], np.float32).reshape(1, -1),
        "Wo": np.asarray(inputs["L0_attn_Wo"], np.float32),
        "bo": np.asarray(inputs["L0_attn_bo"], np.float32).reshape(1, -1),
        "g1": np.ascontiguousarray(np.asarray(inputs["L0_attn_g1"], np.float32).reshape(8, 128).T),
        "b1": np.ascontiguousarray(np.asarray(inputs["L0_attn_b1"], np.float32).reshape(8, 128).T),
        "g2": np.ascontiguousarray(np.asarray(inputs["L0_attn_g2"], np.float32).reshape(8, 128).T),
        "b2": np.ascontiguousarray(np.asarray(inputs["L0_attn_b2"], np.float32).reshape(8, 128).T),
        "Wf1": np.asarray(inputs["L0_attn_Wf1"], np.float32),
        "bf1c": np.ascontiguousarray(np.asarray(inputs["L0_attn_bf1"], np.float32).reshape(32, 128).T),
        "Wf2": np.asarray(inputs["L0_attn_Wf2"], np.float32),
        "bf2": np.asarray(inputs["L0_attn_bf2"], np.float32).reshape(1, -1),
        "Wr": np.asarray(inputs["L0_ffn_w_router"], np.float32),
        "wi2": np.asarray(inputs["L0_ffn_wi"], np.float32).reshape(NE * D, DFF),
        "wo2": np.asarray(inputs["L0_ffn_wo"], np.float32).reshape(NE * DFF, D),
    }
    for nm, v in consts.items():
        base["dc_" + nm] = v
    in_maps = []
    for b in range(8):
        m = dict(base)
        m["x"] = np.ascontiguousarray(x[b])
        in_maps.append(m)
    return in_maps


def kernel(**inputs):
    in_maps = make_in_maps(inputs)
    nc = build(debug=False)
    res = run_bass_kernel_spmd(nc, in_maps, core_ids=list(range(8)))
    return np.stack([np.asarray(r["out"], np.float32) for r in res.results], axis=0)
